# version 16
# baseline (speedup 1.0000x reference)
import numpy as np
import ml_dtypes

import concourse.bass as bass
import concourse.mybir as mybir
from concourse.alu_op_type import AluOpType
from concourse.bass_utils import run_bass_kernel_spmd
from contextlib import ExitStack

F32 = mybir.dt.float32
BF16 = mybir.dt.bfloat16
AF = mybir.ActivationFunctionType
AX = mybir.AxisListType.X
AL = AluOpType

ENGINES = ("tensor", "vector", "scalar", "gpsimd", "sync")
KB = 1024


class Sched:

    def __init__(self, nc, ctx):
        self.nc = nc
        self.ctx = ctx
        self.ops = []

    def op(self, engine, fn, reads=(), writes=(), dma=None):
        self.ops.append(dict(engine=engine, fn=fn, reads=tuple(reads),
                             writes=tuple(writes), dma=dma))

    def build(self, block, final_dma_fams=()):
        nc = self.nc
        n = len(self.ops)
        last_w = {}
        readers = {}
        deps = [None] * n

        def domain(j):
            oj = self.ops[j]
            return ("D", oj["dma"]) if oj["dma"] else ("E", oj["engine"])

        EXCL = {f"ps{i}" for i in range(8)} | {"pb0", "pb1"}
        for i, o in enumerate(self.ops):
            rds = [b for b in o["reads"] if b not in EXCL]
            wrs = list(o["writes"]) + [b for b in o["reads"] if b in EXCL]
            dep = set()
            for b in rds:
                if b in last_w:
                    dep.add(last_w[b])
            for b in wrs:
                if b in last_w:
                    dep.add(last_w[b])
                dep.update(readers.get(b, ()))
            dep.discard(i)
            byd = {}
            for d in dep:
                k = domain(d)
                if k not in byd or d > byd[k]:
                    byd[k] = d
            if o["engine"] == "tensor" and not o["dma"]:
                byd.pop(("E", "tensor"), None)
            deps[i] = sorted(byd.values())
            for b in rds:
                readers.setdefault(b, []).append(i)
            for b in wrs:
                last_w[b] = i
                readers[b] = []

        needed = set()
        for i in range(n):
            for d in deps[i]:
                if not self.ops[d]["dma"]:
                    needed.add(d)

        self.esem = {e: self.ctx.enter_context(nc.semaphore(f"es_{e}"))
                     for e in ENGINES}
        fams = sorted({o["dma"] for o in self.ops if o["dma"]})
        self.dsem = {f: self.ctx.enter_context(nc.semaphore(f"ds_{f}"))
                     for f in fams}
        ecnt = {e: 0 for e in ENGINES}
        dcnt = {f: 0 for f in fams}
        evs = [None] * n
        for i, o in enumerate(self.ops):
            if o["dma"]:
                dcnt[o["dma"]] += 16
                evs[i] = (self.dsem[o["dma"]], dcnt[o["dma"]])
                o["inc"] = True
            elif i in needed:
                ecnt[o["engine"]] += 1
                evs[i] = (self.esem[o["engine"]], ecnt[o["engine"]])
                o["inc"] = True
            else:
                o["inc"] = False
        self.dma_totals = dict(dcnt)
        self.n_incs = len(needed)

        per_engine = {e: [] for e in ENGINES}
        for i, o in enumerate(self.ops):
            per_engine[o["engine"]].append(i)
        sched = self

        def run(eng, engine_name):
            water = {}
            for i in per_engine[engine_name]:
                o = sched.ops[i]
                for d in deps[i]:
                    sem, val = evs[d]
                    k = id(sem)
                    if water.get(k, -1) >= val:
                        continue
                    eng.wait_ge(sem, val)
                    water[k] = val
                inst = o["fn"](eng)
                if o["inc"] and inst is not None:
                    sem, val = evs[i]
                    inst.then_inc(sem, 16 if o["dma"] else 1)
            if engine_name == "sync":
                for f in final_dma_fams:
                    eng.wait_ge(sched.dsem[f], sched.dma_totals[f])

        block.tensor(lambda t: run(t, "tensor"))
        block.vector(lambda v: run(v, "vector"))
        block.scalar(lambda s: run(s, "scalar"))
        block.gpsimd(lambda g: run(g, "gpsimd"))
        block.sync(lambda sy: run(sy, "sync"))


class Overlay:

    def __init__(self, nc, ctx, total_bytes):
        self.nc = nc
        self.back = ctx.enter_context(
            nc.sbuf_tensor("ovl_arena", [128, total_bytes], mybir.dt.uint8))
        self.base = nc.lookup_mloc(self.back).addr
        self.total = total_bytes

    def t(self, name, shape, dtype, off):
        per_part = int(np.prod(shape[1:])) * mybir.dt.size(dtype)
        assert off + per_part <= self.total, (name, off, per_part)
        return self.nc.alloc_sbuf_tensor_at(
            name, list(shape), dtype, offset=self.base + off)


def build_program(na=4, nb=4, bstage=9, astage=9, kproj=9, kelu=15):
    nc = bass.Bass("TRN2", target_bir_lowering=False, debug=False)

    X = nc.dram_tensor("X", [2048, 2048], F32, kind="ExternalInput").ap()
    XT = nc.dram_tensor("XT", [2048, 2048], F32, kind="ExternalInput").ap()
    WQK = nc.dram_tensor("WQK", [8, 128, 4096], F32, kind="ExternalInput").ap()
    WV = nc.dram_tensor("WV", [8, 128, 4096], F32, kind="ExternalInput").ap()
    WU = nc.dram_tensor("WU", [8, 128, 4096], F32, kind="ExternalInput").ap()
    WO = nc.dram_tensor("WO", [8, 128, 4096], F32, kind="ExternalInput").ap()
    GT = nc.dram_tensor("GT", [32, 128, 4096], F32, kind="ExternalInput").ap()
    UT = nc.dram_tensor("UT", [32, 128, 4096], F32, kind="ExternalInput").ap()
    DT = nc.dram_tensor("DT", [16, 128, 8192], F32, kind="ExternalInput").ap()
    R1T = nc.dram_tensor("R1T", [2048, 128], F32, kind="ExternalInput").ap()
    R2T = nc.dram_tensor("R2T", [128, 64], F32, kind="ExternalInput").ap()
    BQK = nc.dram_tensor("BQK", [128, 16], F32, kind="ExternalInput").ap()
    BV = nc.dram_tensor("BV", [128, 16], F32, kind="ExternalInput").ap()
    BU = nc.dram_tensor("BU", [128, 16], F32, kind="ExternalInput").ap()
    BO = nc.dram_tensor("BO", [128, 16], F32, kind="ExternalInput").ap()
    LNG = nc.dram_tensor("LNG", [128, 16], F32, kind="ExternalInput").ap()
    LNB = nc.dram_tensor("LNB", [128, 16], F32, kind="ExternalInput").ap()
    IDB = nc.dram_tensor("IDB", [128, 128], BF16, kind="ExternalInput").ap()
    IDF = nc.dram_tensor("IDF", [128, 128], F32, kind="ExternalInput").ap()
    ONEC = nc.dram_tensor("ONEC", [128, 1], BF16, kind="ExternalInput").ap()
    ONERF = nc.dram_tensor("ONERF", [1, 128], F32, kind="ExternalInput").ap()
    OH = nc.dram_tensor("OH", [64, 8192], BF16, kind="ExternalInput").ap()

    OUT = nc.dram_tensor("OUT", [2048, 2048], F32, kind="ExternalOutput").ap()

    XNSP = nc.dram_tensor("XNSP", [2048, 2048], BF16, kind="Internal").ap()
    KVIB = nc.dram_tensor("KVIB", [128, 2048], F32, kind="Internal")
    KVOB = nc.dram_tensor("KVOB", [128, 2048], F32, kind="Internal")
    QSP = nc.dram_tensor("QSP", [2048, 2048], BF16, kind="Internal").ap()

    with ExitStack() as ctx:
        ov = Overlay(nc, ctx, 204 * KB)
        t = ov.t
        oh_sb = t("oh_sb", [64, 8192], BF16, 0 * KB)
        r1t_sb = t("r1t_sb", [128, 2048], F32, 16 * KB)
        wr0c = t("wr0c", [128, 4096], BF16, 16 * KB)
        mask_fm = t("mask_fm", [64, 2048], BF16, 24 * KB)
        kv_bf = t("kv_bf", [128, 2048], BF16, 28 * KB)
        idf = t("idf", [128, 128], F32, 32 * KB)
        idb = t("idb", [128, 128], BF16, 32 * KB + 512)
        r2t_sb = t("r2t_sb", [128, 64], F32, 32 * KB + 768)
        bqk_c = t("bqk_c", [128, 16], F32, 33 * KB + 0)
        bv_c = t("bv_c", [128, 16], F32, 33 * KB + 64)
        bu_c = t("bu_c", [128, 16], F32, 33 * KB + 128)
        bo_c = t("bo_c", [128, 16], F32, 33 * KB + 192)
        lng_c = t("lng_c", [128, 16], F32, 33 * KB + 256)
        lnb_c = t("lnb_c", [128, 16], F32, 33 * KB + 320)
        onec = t("onec", [128, 1], BF16, 33 * KB + 384)
        onerf = t("onerf", [1, 128], F32, 33 * KB + 512)
        sl1 = t("sl1", [128, 512], F32, 34 * KB)
        t3b = t("t3b", [128, 512], F32, 34 * KB)
        murow = t("murow", [1, 1024], F32, 36 * KB)
        lg4 = t("lg4", [128, 256], F32, 40 * KB)
        sl2 = t("sl2", [64, 512], F32, 41 * KB)
        m8 = t("m8", [128, 8], F32, 43 * KB)
        wkb = t("wkb", [128, 64], F32, 43 * KB + 32)
        wk2 = t("wk2", [128, 64], F32, 43 * KB + 288)
        mtok = t("mtok", [128, 64], BF16, 43 * KB + 544)
        scol = t("scol", [128, 24], F32, 43 * KB + 672)
        xa = t("xa", [128, 8192], F32, 44 * KB)
        aos = t("aos", [128, 8192], F32, 44 * KB)
        rc32 = t("rc32", [128, 8192], F32, 76 * KB)
        qt = t("qt", [128, 8192], BF16, 76 * KB)
        hid0 = t("hid0", [128, 8192], BF16, 76 * KB)
        hid1 = t("hid1", [128, 8192], BF16, 92 * KB)
        xnfm = t("xnfm", [128, 8192], BF16, 108 * KB)
        qfm = t("qfm", [128, 8192], BF16, 124 * KB)
        aobf = t("aobf", [128, 8192], BF16, 124 * KB)
        hid2 = t("hid2", [128, 8192], BF16, 124 * KB)
        vfm = t("vfm", [128, 8192], BF16, 140 * KB)
        kvre = t("kvre", [128, 2048], F32, 124 * KB)
        ybuf = t("ybuf", [128, 8192], BF16, 140 * KB)
        hid3 = t("hid3", [128, 8192], BF16, 140 * KB)
        wr0 = t("wr0", [128, 8192], BF16, 156 * KB)
        wr0a = t("wr0a", [128, 4096], BF16, 156 * KB)
        wr0b = t("wr0b", [128, 4096], BF16, 164 * KB)
        wr1 = t("wr1", [128, 8192], BF16, 172 * KB)
        wr1a = t("wr1a", [128, 4096], BF16, 172 * KB)
        wr1b = t("wr1b", [128, 4096], BF16, 180 * KB)
        qtok = t("qtok", [128, 2048], BF16, 188 * KB)
        exb = t("exb", [128, 512], F32, 188 * KB)
        xtj0 = t("xtj0", [128, 512], F32, 188 * KB)
        xtj1 = t("xtj1", [128, 512], F32, 190 * KB)
        vtok = t("vtok", [128, 2048], BF16, 192 * KB)
        ryb = t("ryb", [128, 512], F32, 192 * KB)
        t3 = t("t3", [128, 512], F32, 192 * KB)
        sqb = [t("sqb0", [128, 512], BF16, 194 * KB),
               t("sqb1", [128, 512], BF16, 195 * KB)]
        kvacc = t("kvacc", [128, 2048], F32, 196 * KB)
        ucur = t("ucur", [128, 512], F32, 196 * KB)
        gch = t("gch", [128, 512], F32, 198 * KB)
        ucur_b = t("ucur_b", [128, 512], F32, 198 * KB)
        och = [t("och0", [128, 512], F32, 200 * KB),
               t("och1", [128, 512], F32, 202 * KB)]

        hid = (hid0, hid1, hid2, hid3)
        KVKEYS = ["RKu", "RKg", "RKo0", "RKo1"]
        QTKEYS = ["RI1a", "RI1b"]
        VTKEYS = ["RI2a", "RI2b0", "RI2b1"]

        ps = [ctx.enter_context(nc.psum_tensor(f"ps{i}", [128, 512], F32))
              for i in range(6)]
        pb = [ctx.enter_context(nc.psum_tensor(f"pq{i}", [128, 1024], BF16))
              for i in range(2)]
        PSK = [f"ps{i}" for i in range(6)]
        PBK = ["pb0", "pb1"]

        s = Sched(nc, ctx)
        O = s.op

        def cl(dst, src, key, fam):
            O("sync", lambda e, dst=dst, src=src: e.dma_start(out=dst[:], in_=src),
              writes=[key], dma=fam)

        cl(oh_sb, OH, "oh", "c_oh")
        cl(idf, IDF, "idf", "c_id")
        cl(idb, IDB, "idb", "c_id2")
        cl(r2t_sb, R2T, "r2t", "c_r2")
        cl(bqk_c, BQK, "bqk", "c_b1")
        cl(bv_c, BV, "bv", "c_b2")
        cl(bu_c, BU, "bu", "c_b3")
        cl(bo_c, BO, "bo", "c_b4")
        cl(lng_c, LNG, "lng", "c_b5")
        cl(lnb_c, LNB, "lnb", "c_b6")
        cl(onec, ONEC, "onec", "c_o1")
        cl(onerf, ONERF, "onerf", "c_o2")
        O("sync", lambda e: e.dma_start(
            out=r1t_sb[:].rearrange("el (eh r) -> el eh r", eh=16),
            in_=R1T.rearrange("(eh el) r -> el eh r", el=128)),
          writes=["W0C"], dma="c_r1")

        O("vector", lambda v: v.memset(kvacc[:], 0.0), writes=KVKEYS)

        inv_e = float(1.0 / 2048.0)
        eps = float(1e-5)

        for tt in range(na):
            own = tt < 4
            xsrc = X[tt * 512:(tt + 1) * 512, :].rearrange(
                "(g p) e -> p g e", p=128)
            xdst = xa[:].rearrange("p (g e) -> p g e", g=4)
            O("sync", lambda e, xsrc=xsrc, xdst=xdst: e.dma_start(
                out=xdst, in_=xsrc),
              writes=["RA"], dma="xa")

            for g in range(4):
                xg = xa[:, g * 2048:(g + 1) * 2048]
                O("vector", lambda v, xg=xg, g=g: v.reduce_sum(
                    scol[:, g:g + 1], xg, AX),
                  reads=["RA"], writes=["scolA"])
                O("scalar", lambda e, xg=xg, g=g: e.activation(
                    rc32[:, g * 2048:(g + 1) * 2048], xg, AF.Square,
                    accum_out=scol[:, 4 + g:5 + g]),
                  reads=["RA"], writes=["RC", "scolB"])
            O("vector", lambda v: v.tensor_scalar_mul(
                scol[:, 8:12], scol[:, 0:4], inv_e),
              reads=["scolA"], writes=["scolC"])
            O("vector", lambda v: v.tensor_scalar_mul(
                scol[:, 12:16], scol[:, 4:8], inv_e),
              reads=["scolB"], writes=["scolD"])
            O("vector", lambda v: v.tensor_tensor(
                scol[:, 16:20], scol[:, 8:12], scol[:, 8:12], AL.mult),
              reads=["scolC"], writes=["scolE"])
            O("vector", lambda v: v.tensor_sub(
                scol[:, 16:20], scol[:, 12:16], scol[:, 16:20]),
              reads=["scolD", "scolE"], writes=["scolE"])
            O("vector", lambda v: v.tensor_scalar_add(
                scol[:, 16:20], scol[:, 16:20], eps),
              reads=["scolE"], writes=["scolE"])
            O("vector", lambda v: v.reciprocal(
                scol[:, 20:24], scol[:, 16:20]),
              reads=["scolE"], writes=["scolF"])
            O("scalar", lambda e: e.activation(
                scol[:, 20:24], scol[:, 20:24], AF.Sqrt),
              reads=["scolF"], writes=["scolF"])
            for g in range(4):
                xg = xa[:, g * 2048:(g + 1) * 2048]
                O("vector", lambda v, xg=xg, g=g: v.tensor_scalar(
                    xg, xg, scol[:, 8 + g:9 + g], scol[:, 20 + g:21 + g],
                    AL.subtract, AL.mult),
                  reads=["RA", "scolC", "scolF"], writes=["RA"])

            for eh in range(16):
                p = ps[eh % 4]
                pk = PSK[eh % 4]
                for g in range(4):
                    src = xa[:, g * 2048 + eh * 128: g * 2048 + (eh + 1) * 128]
                    O("tensor", lambda e, p=p, g=g, src=src: e.transpose(
                        p[:, g * 128:(g + 1) * 128], src, idf[:]),
                      reads=["RA", "idf"], writes=[pk])
                O("vector", lambda v, p=p, eh=eh: v.tensor_copy(
                    xnfm[:, eh * 512:(eh + 1) * 512], p[:]),
                  reads=[pk], writes=["RE"])
                if own:
                    O("scalar", lambda e, p=p, eh=eh: e.activation(
                        rc32[:, eh * 512:(eh + 1) * 512], p[:], AF.Identity),
                      reads=[pk], writes=["RC"])

            if astage < 3:
                continue
            if own:
                dst = XNSP[:, tt * 512:(tt + 1) * 512].rearrange(
                    "(eh el) n -> el eh n", el=128)
                xnv = xnfm[:].rearrange("el (eh n) -> el eh n", eh=16)
                O("sync", lambda e, dst=dst, xnv=xnv: e.dma_start(
                    out=dst, in_=xnv),
                  reads=["RE"], writes=["spx"], dma="spx")

            projs = [(WQK, bqk_c, "elu", qfm, "RD1"),
                     (WV, bv_c, "silu", vfm, "RD2")][:1 if kproj < 4 else 2]
            for (W, bias_c, kind, dst, dkey) in projs:
                for fc in range(8):
                    ring = (wr0a, wr0b)[fc % 2]
                    rkey = ("W0A", "W0B")[fc % 2]
                    rfam = ("w0a", "w0b")[fc % 2]
                    wsrc = W[fc]
                    O("gpsimd", lambda e, wsrc=wsrc, ring=ring: e.dma_start(
                        out=ring[:], in_=wsrc),
                      writes=[rkey], dma=rfam)
                    for f1 in range(2):
                        if kproj < 2:
                            continue
                        f = fc * 2 + f1
                        p = ps[f % 4]
                        pk = PSK[f % 4]
                        for eh in range(16):
                            lhs = ring[:, eh * 256 + f1 * 128:
                                       eh * 256 + (f1 + 1) * 128]
                            rhs = xnfm[:, eh * 512:(eh + 1) * 512]
                            O("tensor", lambda e, p=p, lhs=lhs, rhs=rhs, eh=eh:
                              e.matmul(p[:], lhs, rhs,
                                       start=(eh == 0), stop=(eh == 15)),
                              reads=[rkey, "RE"], writes=[pk])
                        dslice = dst[:, f * 512:(f + 1) * 512]
                        bcol = bias_c[:, f:f + 1]
                        if kproj < 3:
                            O("vector", lambda v, p=p, dslice=dslice: v.tensor_copy(
                                dslice, p[:]),
                              reads=[pk], writes=[dkey])
                        elif kind == "elu" and kelu >= 32:
                            if kelu == 32:
                                O("vector", lambda v, p=p: v.tensor_copy(exb[:], p[:]),
                                  reads=[pk], writes=["RI1a"])
                                O("vector", lambda v, dslice=dslice: v.tensor_copy(
                                    dslice, exb[:]),
                                  reads=["RI1a"], writes=[dkey])
                            elif kelu == 33:
                                O("scalar", lambda e, p=p: e.activation(
                                    ryb[:], p[:], AF.Relu),
                                  reads=[pk], writes=["RI2a"])
                                O("vector", lambda v, p=p, dslice=dslice: v.tensor_copy(
                                    dslice, p[:]),
                                  reads=[pk], writes=[dkey])
                            elif kelu == 34:
                                O("scalar", lambda e, p=p, dslice=dslice: e.activation(
                                    dslice, p[:], AF.Relu),
                                  reads=[pk], writes=[dkey])
                        elif kind == "elu":
                            if kelu & 1:
                                O("vector", lambda v, p=p, bcol=bcol: v.tensor_scalar(
                                    exb[:], p[:], bcol, 0.0, AL.add, AL.min),
                                  reads=[pk, "bqk"], writes=["RI1a"])
                            else:
                                O("vector", lambda v, p=p: v.tensor_copy(
                                    exb[:], p[:]),
                                  reads=[pk], writes=["RI1a"])
                            if kelu & 2:
                                O("scalar", lambda e, p=p, bcol=bcol: e.activation(
                                    ryb[:], p[:], AF.Relu, bias=bcol, scale=1.0),
                                  reads=[pk, "bqk"], writes=["RI2a"])
                            else:
                                O("scalar", lambda e, p=p: e.activation(
                                    ryb[:], p[:], AF.Relu),
                                  reads=[pk], writes=["RI2a"])
                            if kelu & 4:
                                O("scalar", lambda e: e.activation(
                                    exb[:], exb[:], AF.Exp),
                                  reads=["RI1a"], writes=["RI1a"])
                            if kelu & 8:
                                O("vector", lambda v, dslice=dslice:
                                  v.scalar_tensor_tensor(
                                      dslice, exb[:], -1.0, ryb[:], AL.add, AL.add),
                                  reads=["RI1a", "RI2a"], writes=[dkey])
                            else:
                                O("vector", lambda v, dslice=dslice: v.tensor_copy(
                                    dslice, exb[:]),
                                  reads=["RI1a"], writes=[dkey])
                        else:
                            O("scalar", lambda e, p=p, bcol=bcol, dslice=dslice:
                              e.activation(dslice, p[:], AF.Silu,
                                           bias=bcol, scale=1.0),
                              reads=[pk, "bv"], writes=[dkey])

            if own:
                for eh in range(16):
                    O("tensor", lambda e, eh=eh: e.matmul(
                        ps[4][:], r1t_sb[:, eh * 128:(eh + 1) * 128],
                        rc32[:, eh * 512 + 0:eh * 512 + 512],
                        start=(eh == 0), stop=(eh == 15)),
                      reads=["W0C", "RC"], writes=[PSK[4]])
                O("scalar", lambda e: e.activation(sl1[:], ps[4][:], AF.Silu),
                  reads=[PSK[4]], writes=["sl1"])
                O("tensor", lambda e: e.matmul(
                    ps[5][0:64, :], r2t_sb[:], sl1[:], start=True, stop=True),
                  reads=["r2t", "sl1"], writes=[PSK[5]])
                O("vector", lambda v: v.tensor_copy(sl2[:], ps[5][0:64, :]),
                  reads=[PSK[5]], writes=["sl2"])
                for g in range(4):
                    O("tensor", lambda e, g=g: e.transpose(
                        ps[4][:, g * 64:(g + 1) * 64],
                        sl2[0:64, g * 128:(g + 1) * 128], idf[0:64, 0:64]),
                      reads=["sl2", "idf"], writes=[PSK[4]])
                O("vector", lambda v: v.tensor_copy(lg4[:], ps[4][:, 0:256]),
                  reads=[PSK[4]], writes=["lg4"])
                for g in range(4):
                    lg = lg4[:, g * 64:(g + 1) * 64]
                    O("vector", lambda v, lg=lg: v.max(out=m8[:], in_=lg),
                      reads=["lg4"], writes=["m8"])
                    O("vector", lambda v, lg=lg: v.match_replace(
                        out=wkb[:], in_to_replace=m8[:], in_values=lg,
                        imm_value=-1e30),
                      reads=["lg4", "m8"], writes=["wkb", "m8"])
                    O("vector", lambda v: v.max(out=m8[:], in_=wkb[:]),
                      reads=["wkb"], writes=["m8"])
                    O("vector", lambda v: v.match_replace(
                        out=wk2[:], in_to_replace=m8[:], in_values=wkb[:],
                        imm_value=-1e30),
                      reads=["wkb", "m8"], writes=["wk2", "m8"])
                    O("vector", lambda v, lg=lg: v.tensor_tensor(
                        mtok[:], lg, wk2[:], AL.is_gt),
                      reads=["lg4", "wk2"], writes=["mtok"])
                    O("tensor", lambda e, g=g: e.transpose(
                        pb[0][0:64, g * 128:(g + 1) * 128], mtok[:], idb[:]),
                      reads=["mtok", "idb"], writes=[PBK[0]])
                O("vector", lambda v, tt=tt: v.tensor_copy(
                    mask_fm[0:64, tt * 512:(tt + 1) * 512], pb[0][0:64, 0:512]),
                  reads=[PBK[0]], writes=["maskfm"])

            if astage < 4:
                continue
            if own:
                dstq = QSP[:, tt * 512:(tt + 1) * 512].rearrange(
                    "(fh fl) n -> fl fh n", fl=128)
                qv = qfm[:].rearrange("fl (fh n) -> fl fh n", fh=16)
                O("sync", lambda e, dstq=dstq, qv=qv: e.dma_start(
                    out=dstq, in_=qv),
                  reads=["RD1"], writes=["spq"], dma="spq")

            if astage < 5:
                continue
            for g in range(4):
                for (src, dst, skey, dkey) in ((qfm, qtok, "RD1", "RI1"),
                                               (vfm, vtok, "RD2", "RI2")):
                    for half in range(2):
                        pbt = pb[half]
                        pbk = PBK[half]
                        for j in range(8):
                            fh = half * 8 + j
                            sl = src[:, fh * 512 + g * 128:
                                     fh * 512 + (g + 1) * 128]
                            O("tensor", lambda e, pbt=pbt, j=j, sl=sl:
                              e.transpose(pbt[:, j * 128:(j + 1) * 128],
                                          sl, idb[:]),
                              reads=[skey, "idb"], writes=[pbk])
                        O("vector", lambda v, pbt=pbt, dst=dst, half=half:
                          v.tensor_copy(
                              dst[:, half * 1024:(half + 1) * 1024], pbt[:]),
                          reads=[pbk], writes=QTKEYS if dst is qtok else VTKEYS)
                for h in range(16):
                    j = h // 4
                    O("tensor", lambda e, h=h, j=j: e.matmul(
                        ps[2 + j][:, (h % 4) * 128:(h % 4 + 1) * 128],
                        qtok[:, h * 128:(h + 1) * 128],
                        vtok[:, h * 128:(h + 1) * 128],
                        start=True, stop=True),
                      reads=QTKEYS + VTKEYS, writes=[PSK[2 + j]])
                for j in range(4):
                    O("vector", lambda v, j=j: v.tensor_tensor(
                        kvacc[:, j * 512:(j + 1) * 512],
                        kvacc[:, j * 512:(j + 1) * 512],
                        ps[2 + j][:], AL.add),
                      reads=KVKEYS + [PSK[2 + j]], writes=KVKEYS)

        O("sync", lambda e: e.dma_start(out=KVIB.ap(), in_=kvacc[:]),
          reads=KVKEYS, writes=["kvib"], dma="kvx")
        O("gpsimd", lambda g: g.collective_compute(
            "AllReduce", mybir.AluOpType.add,
            replica_groups=[[0, 1], [2, 3], [4, 5], [6, 7]],
            ins=[KVIB.ap().opt()], outs=[KVOB.ap().opt()]),
          reads=["kvib"], writes=["kvob"])
        O("sync", lambda e: e.dma_start(out=kvre[:], in_=KVOB.ap()),
          reads=["kvob"], writes=["RD1"], dma="kvr")
        O("vector", lambda v: v.tensor_copy(kv_bf[:], kvre[:]),
          reads=["RD1"], writes=["kvbf"])

        for tn in range(nb):
            nto = tn * 512
            src = XNSP[:, nto:nto + 512].rearrange("(eh el) n -> el eh n", el=128)
            xnv = xnfm[:].rearrange("el (eh n) -> el eh n", eh=16)
            O("sync", lambda e, src=src, xnv=xnv: e.dma_start(out=xnv, in_=src),
              reads=["spx"], writes=["RE"], dma="rxn")
            srcq = QSP[:, nto:nto + 512].rearrange("(fh fl) n -> fl fh n", fl=128)
            qtv = qt[:].rearrange("fl (fh n) -> fl fh n", fh=16)
            O("sync", lambda e, srcq=srcq, qtv=qtv: e.dma_start(out=qtv, in_=srcq),
              reads=["spq"], writes=["RC"], dma="rq")

            for hh in range(16):
                p = ps[hh % 4]
                pk = PSK[hh % 4]
                O("tensor", lambda e, p=p, hh=hh: e.matmul(
                    p[:], kv_bf[:, hh * 128:(hh + 1) * 128],
                    qt[:, hh * 512:(hh + 1) * 512], start=True, stop=True),
                  reads=["kvbf", "RC"], writes=[pk])
                O("vector", lambda v, p=p, hh=hh: v.tensor_copy(
                    aos[:, hh * 512:(hh + 1) * 512], p[:]),
                  reads=[pk], writes=["RA"])
                O("scalar", lambda e, p=p, hh=hh: e.activation(
                    aobf[:, hh * 512:(hh + 1) * 512], p[:], AF.Identity),
                  reads=[pk], writes=["RD1"])

            if bstage < 2:
                continue
            for eh in range(16):
                O("tensor", lambda e, eh=eh: e.matmul(
                    ps[2][0:1, :], onec[:],
                    aobf[:, eh * 512:(eh + 1) * 512],
                    start=(eh == 0), stop=(eh == 15)),
                  reads=["onec", "RD1"], writes=[PSK[2]])
            O("vector", lambda v: v.tensor_scalar_mul(
                murow[0:1, 0:512], ps[2][0:1, :], inv_e),
              reads=[PSK[2]], writes=["murow"])
            O("tensor", lambda e: e.matmul(
                ps[4][:], onerf[:], murow[0:1, 0:512], start=True, stop=True),
              reads=["onerf", "murow"], writes=[PSK[4]])
            ucur4 = (ucur, ucur_b, och[0], och[1])
            uck4 = ("RKu", "RKg", "RKo0", "RKo1")

            def u_group(f):
                uc, f1 = f // 2, f % 2
                ring = (wr0a, wr0b, wr0c)[uc % 3]
                rkey = ("W0A", "W0B", "W0C")[uc % 3]
                rfam = ("w0a", "w0b", "w0c")[uc % 3]
                if f1 == 0:
                    wsrc = WU[uc]
                    O("gpsimd", lambda e, wsrc=wsrc, ring=ring: e.dma_start(
                        out=ring[:], in_=wsrc),
                      writes=[rkey], dma=rfam)
                p = ps[f % 4]
                pk = PSK[f % 4]
                for eh in range(16):
                    lhs = ring[:, eh * 256 + f1 * 128:eh * 256 + (f1 + 1) * 128]
                    O("tensor", lambda e, p=p, lhs=lhs, eh=eh: e.matmul(
                        p[:], lhs, xnfm[:, eh * 512:(eh + 1) * 512],
                        start=(eh == 0), stop=(eh == 15)),
                      reads=[rkey, "RE"], writes=[pk])
                O("scalar", lambda e, p=p, f=f: e.activation(
                    ucur4[f % 4][:], p[:], AF.Silu,
                    bias=bu_c[:, f:f + 1], scale=1.0),
                  reads=[pk, "bu"], writes=[uck4[f % 4]])

            def y_group(f):
                tc = (t3, t3b)[f % 2]
                tck = ("RI2a", "sl1")[f % 2]
                O("vector", lambda v, f=f, tc=tc: v.tensor_tensor(
                    tc[:], aos[:, f * 512:(f + 1) * 512], ps[4][:],
                    AL.subtract),
                  reads=["RA", PSK[4]], writes=[tck])
                O("vector", lambda v, tc=tc: v.tensor_tensor(
                    tc[:], tc[:], ps[5][:], AL.mult),
                  reads=[tck, PSK[5]], writes=[tck])
                O("scalar", lambda e, f=f, tc=tc: e.activation(
                    tc[:], tc[:], AF.Identity,
                    bias=lnb_c[:, f:f + 1], scale=lng_c[:, f:f + 1]),
                  reads=[tck, "lng", "lnb"], writes=[tck])
                O("vector", lambda v, f=f, tc=tc: v.tensor_tensor(
                    ybuf[:, f * 512:(f + 1) * 512], tc[:], ucur4[f % 4][:],
                    AL.mult),
                  reads=[tck, uck4[f % 4]], writes=["RD2"])

            for f in range(4):
                u_group(f)
            for eh in range(16):
                sq = sqb[eh % 2]
                sqk = ("RI2b0", "RI2b1")[eh % 2]
                O("vector", lambda v, eh=eh, sq=sq: v.tensor_tensor(
                    sq[:], aos[:, eh * 512:(eh + 1) * 512], ps[4][:],
                    AL.subtract),
                  reads=["RA", PSK[4]], writes=[sqk])
                O("scalar", lambda e, sq=sq: e.activation(
                    sq[:], sq[:], AF.Square),
                  reads=[sqk], writes=[sqk])
                O("tensor", lambda e, sq=sq, eh=eh: e.matmul(
                    ps[3][0:1, :], onec[:], sq[:],
                    start=(eh == 0), stop=(eh == 15)),
                  reads=["onec", sqk], writes=[PSK[3]])
            O("vector", lambda v: v.tensor_scalar(
                murow[0:1, 512:1024], ps[3][0:1, :], inv_e, eps,
                AL.mult, AL.add),
              reads=[PSK[3]], writes=["murow"])
            O("vector", lambda v: v.reciprocal(
                murow[0:1, 512:1024], murow[0:1, 512:1024]),
              reads=["murow"], writes=["murow"])
            O("scalar", lambda e: e.activation(
                murow[0:1, 512:1024], murow[0:1, 512:1024], AF.Sqrt),
              reads=["murow"], writes=["murow"])
            O("tensor", lambda e: e.matmul(
                ps[5][:], onerf[:], murow[0:1, 512:1024], start=True, stop=True),
              reads=["onerf", "murow"], writes=[PSK[5]])
            for f in range(4):
                y_group(f)
            for f in range(4, 16):
                u_group(f)
                y_group(f)
            def ffn_chunks(ic_lo, ic_hi):
              for ic in range(ic_lo, ic_hi):
                  gring = (wr0a, wr0b, wr0c)[ic % 3]
                  gkey = ("W0A", "W0B", "W0C")[ic % 3]
                  gfam = ("w0a", "w0b", "w0c")[ic % 3]
                  uring = (wr1a, wr1b)[ic % 2]
                  ukey = ("W1A", "W1B")[ic % 2]
                  ufam = ("w1a", "w1b")[ic % 2]
                  g_src = GT[ic]
                  u_src = UT[ic]
                  O("gpsimd", lambda e, g_src=g_src, gring=gring: e.dma_start(
                      out=gring[:], in_=g_src),
                    writes=[gkey], dma=gfam)
                  O("gpsimd", lambda e, u_src=u_src, uring=uring: e.dma_start(
                      out=uring[:], in_=u_src),
                    writes=[ukey], dma=ufam)
                  for isub in range(2):
                      blk = ic * 2 + isub
                      pg = ps[blk % 2]
                      pgk = PSK[blk % 2]
                      pu = ps[2 + blk % 2]
                      puk = PSK[2 + blk % 2]
                      pm = ps[4 + blk % 2]
                      pmk = PSK[4 + blk % 2]
                      for eh in range(16):
                          lhs = gring[:, eh * 256 + isub * 128:
                                      eh * 256 + (isub + 1) * 128]
                          O("tensor", lambda e, pg=pg, lhs=lhs, eh=eh: e.matmul(
                              pg[:], lhs, xnfm[:, eh * 512:(eh + 1) * 512],
                              start=(eh == 0), stop=(eh == 15)),
                            reads=[gkey, "RE"], writes=[pgk])
                      for eh in range(16):
                          lhs = uring[:, eh * 256 + isub * 128:
                                      eh * 256 + (isub + 1) * 128]
                          O("tensor", lambda e, pu=pu, lhs=lhs, eh=eh: e.matmul(
                              pu[:], lhs, xnfm[:, eh * 512:(eh + 1) * 512],
                              start=(eh == 0), stop=(eh == 15)),
                            reads=[ukey, "RE"], writes=[puk])
                      O("tensor", lambda e, pm=pm, blk=blk, tn=tn: e.matmul(
                          pm[:], oh_sb[0:64, blk * 128:(blk + 1) * 128],
                          mask_fm[0:64, tn * 512:(tn + 1) * 512],
                          start=True, stop=True),
                        reads=["oh", "maskfm"], writes=[pmk])
                      O("scalar", lambda e, pg=pg: e.activation(
                          gch[:], pg[:], AF.Silu),
                        reads=[pgk], writes=["RKg"])
                      O("vector", lambda v, pu=pu: v.tensor_tensor(
                          t3[:], gch[:], pu[:], AL.mult),
                        reads=["RKg", puk], writes=["RI2a"])
                      hdst = hid[blk // 16][:, (blk % 16) * 512:
                                            (blk % 16 + 1) * 512]
                      hkey = ("RC", "RC", "RD1", "RD2")[blk // 16]
                      O("vector", lambda v, pm=pm, hdst=hdst: v.tensor_tensor(
                          hdst, t3[:], pm[:], AL.mult),
                        reads=["RI2a", pmk], writes=[hkey])

            if bstage < 6:
                  continue

            ffn_chunks(0, 16)
            for ec in range(8):
                ring = (wr0a, wr0b, wr0c)[ec % 3]
                rkey = ("W0A", "W0B", "W0C")[ec % 3]
                rfam = ("w0a", "w0b", "w0c")[ec % 3]
                wsrc = WO[ec]
                O("gpsimd", lambda e, wsrc=wsrc, ring=ring: e.dma_start(
                    out=ring[:], in_=wsrc),
                  writes=[rkey], dma=rfam)
                for e1 in range(2):
                    ei = ec * 2 + e1
                    p = ps[ei % 4]
                    pk = PSK[ei % 4]
                    for hh in range(16):
                        lhs = ring[:, hh * 256 + e1 * 128:hh * 256 + (e1 + 1) * 128]
                        O("tensor", lambda e, p=p, lhs=lhs, hh=hh: e.matmul(
                            p[:], lhs, ybuf[:, hh * 512:(hh + 1) * 512],
                            start=(hh == 0), stop=(hh == 15)),
                          reads=[rkey, "RD2"], writes=[pk])
                    O("scalar", lambda e, p=p, ei=ei: e.activation(
                        aos[:, ei * 512:(ei + 1) * 512], p[:], AF.Identity,
                        bias=bo_c[:, ei:ei + 1], scale=1.0),
                      reads=[pk, "bo"], writes=["RA"])

            if bstage < 5:
                continue
            ffn_chunks(16, 32)

            for d in range(16):
                dring = (wr0, wr1)[d % 2]
                dkeys = (["W0A", "W0B"], ["W1A", "W1B"])[d % 2]
                dfam = ("dna", "dnb")[d % 2]
                dsrc = DT[d]
                O("gpsimd", lambda e, dsrc=dsrc, dring=dring: e.dma_start(
                    out=dring[:], in_=dsrc),
                  writes=dkeys, dma=dfam)
                xsrc = XT[d * 128:(d + 1) * 128, nto:nto + 512]
                xj = (xtj0, xtj1)[d % 2]
                O("sync", lambda e, xsrc=xsrc, xj=xj: e.dma_start(
                    out=xj[:], in_=xsrc),
                  writes=[("RI1a", "RI1b")[d % 2]], dma=("xtj0", "xtj1")[d % 2])
                p = ps[d % 4]
                pk = PSK[d % 4]
                for ih in range(64):
                    hsl = hid[ih // 16][:, (ih % 16) * 512:(ih % 16 + 1) * 512]
                    hkey = ("RC", "RC", "RD1", "RD2")[ih // 16]
                    O("tensor", lambda e, p=p, ih=ih, hsl=hsl, dring=dring:
                      e.matmul(p[:], dring[:, ih * 128:(ih + 1) * 128], hsl,
                               start=(ih == 0), stop=(ih == 63)),
                      reads=dkeys + [hkey], writes=[pk])
                O("scalar", lambda e, p=p: e.activation(gch[:], p[:], AF.Silu),
                  reads=[pk], writes=["RKg"])
                oc = och[d % 2]
                ok = ("RKo0", "RKo1")[d % 2]
                xjk = ("RI1a", "RI1b")[d % 2]
                O("vector", lambda v, oc=oc, d=d: v.tensor_tensor(
                    oc[:], aos[:, d * 512:(d + 1) * 512], gch[:], AL.mult),
                  reads=["RA", "RKg"], writes=[ok])
                O("vector", lambda v, oc=oc, xj=xj: v.tensor_tensor(
                    oc[:], oc[:], xj[:], AL.add),
                  reads=[ok, xjk], writes=[ok])
                odst = OUT[d * 128:(d + 1) * 128, nto:nto + 512]
                O("sync", lambda e, odst=odst, oc=oc: e.dma_start(
                    out=odst, in_=oc[:]),
                  reads=[ok], writes=[f"outd{d % 2}"], dma=("out0", "out1")[d % 2])

        blk = ctx.enter_context(nc.Block())
        fams = tuple(f for f in ("out0", "out1") if any(o["dma"] == f for o in s.ops))
        s.build(blk, final_dma_fams=fams)

    return nc, s


_CACHE = {}


def make_in_maps(inputs, ncores=8):
    f32 = np.float32
    x = np.ascontiguousarray(np.asarray(inputs["L0_attn_x"], f32))
    def chunkw(w, nch, cw):
        w = np.asarray(w, f32)
        kh = w.shape[0] // 128
        return np.ascontiguousarray(
            w.reshape(kh, 128, nch, cw).transpose(2, 1, 0, 3).reshape(
                nch, 128, kh * cw))

    wqk = chunkw(inputs["L0_attn_Wqk"], 8, 256)
    wv = chunkw(inputs["L0_attn_Wv"], 8, 256)
    wu = chunkw(inputs["L0_attn_Wu"], 8, 256)
    wo = chunkw(inputs["L0_attn_Wo"], 8, 256)
    gateT = chunkw(np.asarray(inputs["L0_ffn_gate_w"], f32).T, 32, 256)
    upT = chunkw(np.asarray(inputs["L0_ffn_up_w"], f32).T, 32, 256)
    downT = chunkw(np.asarray(inputs["L0_ffn_down_w"], f32).T, 16, 128)
    r1T = np.ascontiguousarray(np.asarray(inputs["L0_ffn_router_w1"], f32).T)
    r2T = np.ascontiguousarray(np.asarray(inputs["L0_ffn_router_w2"], f32).T)

    def col(v):
        return np.ascontiguousarray(np.asarray(v, f32).reshape(16, 128).T)

    consts = {
        "WQK": wqk, "WV": wv, "WU": wu, "WO": wo,
        "GT": gateT, "UT": upT, "DT": downT, "R1T": r1T, "R2T": r2T,
        "BQK": col(inputs["L0_attn_bqk"]), "BV": col(inputs["L0_attn_bv"]),
        "BU": col(inputs["L0_attn_bu"]), "BO": col(inputs["L0_attn_bo"]),
        "LNG": col(inputs["L0_attn_ln_g"]), "LNB": col(inputs["L0_attn_ln_b"]),
        "IDB": np.eye(128, dtype=ml_dtypes.bfloat16),
        "IDF": np.eye(128, dtype=f32),
        "ONEC": np.ones((128, 1), dtype=ml_dtypes.bfloat16),
        "ONERF": np.ones((1, 128), dtype=f32),
        "OH": np.kron(np.eye(64), np.ones((1, 128))).astype(ml_dtypes.bfloat16),
    }

    in_maps = []
    for c in range(ncores):
        b, h = c >> 1, c & 1
        own = x[b, h * 2048:(h + 1) * 2048]
        m = dict(consts)
        m["X"] = np.ascontiguousarray(own)
        m["XT"] = np.ascontiguousarray(own.T)
        in_maps.append(m)
    return in_maps


def kernel(**inputs):
    in_maps = make_in_maps(inputs)
    if "nc" not in _CACHE:
        _CACHE["nc"] = build_program()
    nc, _ = _CACHE["nc"]

    res = run_bass_kernel_spmd(nc, in_maps, core_ids=list(range(8)))

    out = np.empty((4, 4096, 2048), np.float32)
    for c, r in enumerate(res.results):
        b, h = c >> 1, c & 1
        out[b, h * 2048:(h + 1) * 2048, :] = r["OUT"].T
    return out


if __name__ == "__main__":
    nc, s = build_program()
    print("ops:", len(s.ops), "incs:", s.n_incs)


# version 17
# speedup vs baseline: 1.0744x; 1.0744x over previous
import numpy as np
import ml_dtypes

import concourse.bass as bass
import concourse.mybir as mybir
from concourse.alu_op_type import AluOpType
from concourse.bass_utils import run_bass_kernel_spmd
from contextlib import ExitStack

F32 = mybir.dt.float32
BF16 = mybir.dt.bfloat16
AF = mybir.ActivationFunctionType
AX = mybir.AxisListType.X
AL = AluOpType

ENGINES = ("tensor", "vector", "scalar", "gpsimd", "sync")
KB = 1024


class Sched:

    def __init__(self, nc, ctx):
        self.nc = nc
        self.ctx = ctx
        self.ops = []

    def op(self, engine, fn, reads=(), writes=(), dma=None):
        self.ops.append(dict(engine=engine, fn=fn, reads=tuple(reads),
                             writes=tuple(writes), dma=dma))

    def build(self, block, final_dma_fams=()):
        nc = self.nc
        n = len(self.ops)
        last_w = {}
        readers = {}
        deps = [None] * n

        def domain(j):
            oj = self.ops[j]
            return ("D", oj["dma"]) if oj["dma"] else ("E", oj["engine"])

        EXCL = {f"ps{i}" for i in range(8)} | {"pb0", "pb1"}
        for i, o in enumerate(self.ops):
            rds = [b for b in o["reads"] if b not in EXCL]
            wrs = list(o["writes"]) + [b for b in o["reads"] if b in EXCL]
            dep = set()
            for b in rds:
                if b in last_w:
                    dep.add(last_w[b])
            for b in wrs:
                if b in last_w:
                    dep.add(last_w[b])
                dep.update(readers.get(b, ()))
            dep.discard(i)
            byd = {}
            for d in dep:
                k = domain(d)
                if k not in byd or d > byd[k]:
                    byd[k] = d
            if o["engine"] == "tensor" and not o["dma"]:
                byd.pop(("E", "tensor"), None)
            deps[i] = sorted(byd.values())
            for b in rds:
                readers.setdefault(b, []).append(i)
            for b in wrs:
                last_w[b] = i
                readers[b] = []

        needed = set()
        for i in range(n):
            for d in deps[i]:
                if not self.ops[d]["dma"]:
                    needed.add(d)

        self.esem = {e: self.ctx.enter_context(nc.semaphore(f"es_{e}"))
                     for e in ENGINES}
        fams = sorted({o["dma"] for o in self.ops if o["dma"]})
        self.dsem = {f: self.ctx.enter_context(nc.semaphore(f"ds_{f}"))
                     for f in fams}
        ecnt = {e: 0 for e in ENGINES}
        dcnt = {f: 0 for f in fams}
        evs = [None] * n
        for i, o in enumerate(self.ops):
            if o["dma"]:
                dcnt[o["dma"]] += 16
                evs[i] = (self.dsem[o["dma"]], dcnt[o["dma"]])
                o["inc"] = True
            elif i in needed:
                ecnt[o["engine"]] += 1
                evs[i] = (self.esem[o["engine"]], ecnt[o["engine"]])
                o["inc"] = True
            else:
                o["inc"] = False
        self.dma_totals = dict(dcnt)
        self.n_incs = len(needed)

        per_engine = {e: [] for e in ENGINES}
        for i, o in enumerate(self.ops):
            per_engine[o["engine"]].append(i)
        sched = self

        def run(eng, engine_name):
            water = {}
            for i in per_engine[engine_name]:
                o = sched.ops[i]
                for d in deps[i]:
                    sem, val = evs[d]
                    k = id(sem)
                    if water.get(k, -1) >= val:
                        continue
                    eng.wait_ge(sem, val)
                    water[k] = val
                inst = o["fn"](eng)
                if o["inc"] and inst is not None:
                    sem, val = evs[i]
                    inst.then_inc(sem, 16 if o["dma"] else 1)
            if engine_name == "sync":
                for f in final_dma_fams:
                    eng.wait_ge(sched.dsem[f], sched.dma_totals[f])

        block.tensor(lambda t: run(t, "tensor"))
        block.vector(lambda v: run(v, "vector"))
        block.scalar(lambda s: run(s, "scalar"))
        block.gpsimd(lambda g: run(g, "gpsimd"))
        block.sync(lambda sy: run(sy, "sync"))


class Overlay:

    def __init__(self, nc, ctx, total_bytes):
        self.nc = nc
        self.back = ctx.enter_context(
            nc.sbuf_tensor("ovl_arena", [128, total_bytes], mybir.dt.uint8))
        self.base = nc.lookup_mloc(self.back).addr
        self.total = total_bytes

    def t(self, name, shape, dtype, off):
        per_part = int(np.prod(shape[1:])) * mybir.dt.size(dtype)
        assert off + per_part <= self.total, (name, off, per_part)
        return self.nc.alloc_sbuf_tensor_at(
            name, list(shape), dtype, offset=self.base + off)


def build_program(na=4, nb=4, bstage=9, astage=9, kproj=9, kelu=15):
    nc = bass.Bass("TRN2", target_bir_lowering=False, debug=False)

    X = nc.dram_tensor("X", [2048, 2048], F32, kind="ExternalInput").ap()
    XT = nc.dram_tensor("XT", [2048, 2048], F32, kind="ExternalInput").ap()
    WQK = nc.dram_tensor("WQK", [8, 128, 4096], F32, kind="ExternalInput").ap()
    WV = nc.dram_tensor("WV", [8, 128, 4096], F32, kind="ExternalInput").ap()
    WU = nc.dram_tensor("WU", [8, 128, 4096], F32, kind="ExternalInput").ap()
    WO = nc.dram_tensor("WO", [8, 128, 4096], F32, kind="ExternalInput").ap()
    GT = nc.dram_tensor("GT", [32, 128, 4096], F32, kind="ExternalInput").ap()
    UT = nc.dram_tensor("UT", [32, 128, 4096], F32, kind="ExternalInput").ap()
    DT = nc.dram_tensor("DT", [16, 128, 8192], F32, kind="ExternalInput").ap()
    R1T = nc.dram_tensor("R1T", [2048, 128], F32, kind="ExternalInput").ap()
    R2T = nc.dram_tensor("R2T", [128, 64], F32, kind="ExternalInput").ap()
    BQK = nc.dram_tensor("BQK", [128, 16], F32, kind="ExternalInput").ap()
    BV = nc.dram_tensor("BV", [128, 16], F32, kind="ExternalInput").ap()
    BU = nc.dram_tensor("BU", [128, 16], F32, kind="ExternalInput").ap()
    BO = nc.dram_tensor("BO", [128, 16], F32, kind="ExternalInput").ap()
    LNG = nc.dram_tensor("LNG", [128, 16], F32, kind="ExternalInput").ap()
    LNB = nc.dram_tensor("LNB", [128, 16], F32, kind="ExternalInput").ap()
    IDB = nc.dram_tensor("IDB", [128, 128], BF16, kind="ExternalInput").ap()
    IDF = nc.dram_tensor("IDF", [128, 128], F32, kind="ExternalInput").ap()
    ONEC = nc.dram_tensor("ONEC", [128, 1], BF16, kind="ExternalInput").ap()
    ONERF = nc.dram_tensor("ONERF", [1, 128], F32, kind="ExternalInput").ap()
    OH = nc.dram_tensor("OH", [64, 8192], BF16, kind="ExternalInput").ap()

    OUT = nc.dram_tensor("OUT", [2048, 2048], F32, kind="ExternalOutput").ap()

    XNSP = nc.dram_tensor("XNSP", [2048, 2048], BF16, kind="Internal").ap()
    KVIB = nc.dram_tensor("KVIB", [128, 2048], F32, kind="Internal")
    KVOB = nc.dram_tensor("KVOB", [128, 2048], F32, kind="Internal")
    QSP = nc.dram_tensor("QSP", [2048, 2048], BF16, kind="Internal").ap()

    with ExitStack() as ctx:
        ov = Overlay(nc, ctx, 204 * KB)
        t = ov.t
        oh_sb = t("oh_sb", [64, 8192], BF16, 0 * KB)
        r1t_sb = t("r1t_sb", [128, 2048], F32, 16 * KB)
        mask_fm = t("mask_fm", [64, 2048], BF16, 24 * KB)
        kv_bf = t("kv_bf", [128, 2048], BF16, 28 * KB)
        idf = t("idf", [128, 128], F32, 32 * KB)
        idb = t("idb", [128, 128], BF16, 32 * KB + 512)
        r2t_sb = t("r2t_sb", [128, 64], F32, 32 * KB + 768)
        bqk_c = t("bqk_c", [128, 16], F32, 33 * KB + 0)
        bv_c = t("bv_c", [128, 16], F32, 33 * KB + 64)
        bu_c = t("bu_c", [128, 16], F32, 33 * KB + 128)
        bo_c = t("bo_c", [128, 16], F32, 33 * KB + 192)
        lng_c = t("lng_c", [128, 16], F32, 33 * KB + 256)
        lnb_c = t("lnb_c", [128, 16], F32, 33 * KB + 320)
        onec = t("onec", [128, 1], BF16, 33 * KB + 384)
        onerf = t("onerf", [1, 128], F32, 33 * KB + 512)
        sl1 = t("sl1", [128, 512], F32, 34 * KB)
        t3b = t("t3b", [128, 512], F32, 34 * KB)
        murow = t("murow", [1, 1024], F32, 36 * KB)
        lg4 = t("lg4", [128, 256], F32, 40 * KB)
        sl2 = t("sl2", [64, 512], F32, 41 * KB)
        m8 = t("m8", [128, 8], F32, 43 * KB)
        wkb = t("wkb", [128, 64], F32, 43 * KB + 32)
        wk2 = t("wk2", [128, 64], F32, 43 * KB + 288)
        mtok = t("mtok", [128, 64], BF16, 43 * KB + 544)
        scol = t("scol", [128, 40], F32, 43 * KB + 672)
        xa = t("xa", [128, 8192], F32, 44 * KB)
        aos = t("aos", [128, 8192], F32, 44 * KB)
        rc32 = t("rc32", [128, 8192], F32, 76 * KB)
        qt = t("qt", [128, 8192], BF16, 76 * KB)
        hid0 = t("hid0", [128, 8192], BF16, 76 * KB)
        hid1 = t("hid1", [128, 8192], BF16, 92 * KB)
        xnfm = t("xnfm", [128, 8192], BF16, 108 * KB)
        qfm = t("qfm", [128, 8192], BF16, 124 * KB)
        aobf = t("aobf", [128, 8192], BF16, 124 * KB)
        hid2 = t("hid2", [128, 8192], BF16, 124 * KB)
        vfm = t("vfm", [128, 8192], BF16, 140 * KB)
        ybuf = t("ybuf", [128, 8192], BF16, 140 * KB)
        hid3 = t("hid3", [128, 8192], BF16, 140 * KB)
        wr0 = t("wr0", [128, 8192], BF16, 156 * KB)
        wr0a = t("wr0a", [128, 4096], BF16, 156 * KB)
        wr0b = t("wr0b", [128, 4096], BF16, 164 * KB)
        wr1 = t("wr1", [128, 8192], BF16, 172 * KB)
        wr1a = t("wr1a", [128, 4096], BF16, 172 * KB)
        wr1b = t("wr1b", [128, 4096], BF16, 180 * KB)
        qtok = t("qtok", [128, 2048], BF16, 188 * KB)
        exb = t("exb", [128, 512], F32, 188 * KB)
        xtj0 = t("xtj0", [128, 512], F32, 188 * KB)
        xtj1 = t("xtj1", [128, 512], F32, 190 * KB)
        vtok = t("vtok", [128, 2048], BF16, 192 * KB)
        ryb = t("ryb", [128, 512], F32, 192 * KB)
        t3 = t("t3", [128, 512], F32, 192 * KB)
        sqb = [t("sqb0", [128, 512], BF16, 194 * KB),
               t("sqb1", [128, 512], BF16, 195 * KB)]
        kvacc = t("kvacc", [128, 2048], F32, 196 * KB)
        ucur = t("ucur", [128, 512], F32, 196 * KB)
        gch = t("gch", [128, 512], F32, 198 * KB)
        ucur_b = t("ucur_b", [128, 512], F32, 198 * KB)
        och = [t("och0", [128, 512], F32, 200 * KB),
               t("och1", [128, 512], F32, 202 * KB)]

        hid = (hid0, hid1, hid2, hid3)
        KVKEYS = ["RKu", "RKg", "RKo0", "RKo1"]
        QTKEYS = ["RI1a", "RI1b"]
        VTKEYS = ["RI2a", "RI2b0", "RI2b1"]

        ps = [ctx.enter_context(nc.psum_tensor(f"ps{i}", [128, 512], F32))
              for i in range(6)]
        pb = [ctx.enter_context(nc.psum_tensor(f"pq{i}", [128, 1024], BF16))
              for i in range(2)]
        PSK = [f"ps{i}" for i in range(6)]
        PBK = ["pb0", "pb1"]

        s = Sched(nc, ctx)
        O = s.op

        def cl(dst, src, key, fam):
            O("sync", lambda e, dst=dst, src=src: e.dma_start(out=dst[:], in_=src),
              writes=[key], dma=fam)

        cl(oh_sb, OH, "oh", "c_oh")
        cl(idf, IDF, "idf", "c_id")
        cl(idb, IDB, "idb", "c_id2")
        cl(r2t_sb, R2T, "r2t", "c_r2")
        cl(bqk_c, BQK, "bqk", "c_b1")
        cl(bv_c, BV, "bv", "c_b2")
        cl(bu_c, BU, "bu", "c_b3")
        cl(bo_c, BO, "bo", "c_b4")
        cl(lng_c, LNG, "lng", "c_b5")
        cl(lnb_c, LNB, "lnb", "c_b6")
        cl(onec, ONEC, "onec", "c_o1")
        cl(onerf, ONERF, "onerf", "c_o2")
        O("sync", lambda e: e.dma_start(
            out=r1t_sb[:].rearrange("el (eh r) -> el eh r", eh=16),
            in_=R1T.rearrange("(eh el) r -> el eh r", el=128)),
          writes=["r1t"], dma="c_r1")

        O("vector", lambda v: v.memset(kvacc[:], 0.0), writes=KVKEYS)

        inv_e = float(1.0 / 2048.0)
        eps = float(1e-5)

        def a_load(tt):
            xsrc = X[tt * 512:(tt + 1) * 512, :].rearrange(
                "(g p) e -> p g e", p=128)
            xdst = xa[:].rearrange("p (g e) -> p g e", g=4)
            O("sync", lambda e, xsrc=xsrc, xdst=xdst: e.dma_start(
                out=xdst, in_=xsrc),
              writes=["RA"], dma="xa")


        def a_ln(tt):
            for g in range(4):
                for q in range(4):
                    O("vector", lambda v, g=g, q=q: v.bn_stats(
                        scol[:, q * 6:(q + 1) * 6],
                        xa[:, g * 2048 + q * 512:g * 2048 + (q + 1) * 512]),
                      reads=["RA"], writes=["scolA"])
                O("vector", lambda v, g=g: v.bn_aggr(
                    scol[:, 24 + 2 * g:26 + 2 * g], scol[:, 0:24]),
                  reads=["scolA"], writes=["scolB"])
            O("vector", lambda v: v.tensor_scalar(
                scol[:, 32:36], scol[:, 25:32:2], eps, None, AL.add),
              reads=["scolB"], writes=["scolC"])
            O("vector", lambda v: v.reciprocal(scol[:, 32:36], scol[:, 32:36]),
              reads=["scolC"], writes=["scolC"])
            O("scalar", lambda e: e.activation(
                scol[:, 32:36], scol[:, 32:36], AF.Sqrt),
              reads=["scolC"], writes=["scolC"])
            for g in range(4):
                xg = xa[:, g * 2048:(g + 1) * 2048]
                O("vector", lambda v, xg=xg, g=g: v.tensor_scalar(
                    xg, xg, scol[:, 24 + 2 * g:25 + 2 * g],
                    scol[:, 32 + g:33 + g], AL.subtract, AL.mult),
                  reads=["RA", "scolB", "scolC"], writes=["RA"])

        def a_tr(tt):
            own = tt < 4
            for eh in range(16):
                p = ps[eh % 4]
                pk = PSK[eh % 4]
                for g in range(4):
                    src = xa[:, g * 2048 + eh * 128: g * 2048 + (eh + 1) * 128]
                    O("tensor", lambda e, p=p, g=g, src=src: e.transpose(
                        p[:, g * 128:(g + 1) * 128], src, idf[:]),
                      reads=["RA", "idf"], writes=[pk])
                O("vector", lambda v, p=p, eh=eh: v.tensor_copy(
                    xnfm[:, eh * 512:(eh + 1) * 512], p[:]),
                  reads=[pk], writes=["RE"])
                if own:
                    O("scalar", lambda e, p=p, eh=eh: e.activation(
                        rc32[:, eh * 512:(eh + 1) * 512], p[:], AF.Identity),
                      reads=[pk], writes=["RC"])

            if astage < 3:
                return
            if own:
                dst = XNSP[:, tt * 512:(tt + 1) * 512].rearrange(
                    "(eh el) n -> el eh n", el=128)
                xnv = xnfm[:].rearrange("el (eh n) -> el eh n", eh=16)
                O("sync", lambda e, dst=dst, xnv=xnv: e.dma_start(
                    out=dst, in_=xnv),
                  reads=["RE"], writes=["spx"], dma="spx")


        def a_rest(tt):
            own = tt < 4
            projs = [(WQK, bqk_c, "elu", qfm, "RD1"),
                     (WV, bv_c, "silu", vfm, "RD2")][:1 if kproj < 4 else 2]
            for (W, bias_c, kind, dst, dkey) in projs:
                for fc in range(8):
                    ring = (wr0a, wr0b)[fc % 2]
                    rkey = ("W0A", "W0B")[fc % 2]
                    rfam = ("w0a", "w0b")[fc % 2]
                    wsrc = W[fc]
                    O("gpsimd", lambda e, wsrc=wsrc, ring=ring: e.dma_start(
                        out=ring[:], in_=wsrc),
                      writes=[rkey], dma=rfam)
                    for f1 in range(2):
                        if kproj < 2:
                            return
                        f = fc * 2 + f1
                        p = ps[f % 4]
                        pk = PSK[f % 4]
                        for eh in range(16):
                            lhs = ring[:, eh * 256 + f1 * 128:
                                       eh * 256 + (f1 + 1) * 128]
                            rhs = xnfm[:, eh * 512:(eh + 1) * 512]
                            O("tensor", lambda e, p=p, lhs=lhs, rhs=rhs, eh=eh:
                              e.matmul(p[:], lhs, rhs,
                                       start=(eh == 0), stop=(eh == 15)),
                              reads=[rkey, "RE"], writes=[pk])
                        dslice = dst[:, f * 512:(f + 1) * 512]
                        bcol = bias_c[:, f:f + 1]
                        if kproj < 3:
                            O("vector", lambda v, p=p, dslice=dslice: v.tensor_copy(
                                dslice, p[:]),
                              reads=[pk], writes=[dkey])
                        elif kind == "elu" and kelu >= 32:
                            if kelu == 32:
                                O("vector", lambda v, p=p: v.tensor_copy(exb[:], p[:]),
                                  reads=[pk], writes=["RI1a"])
                                O("vector", lambda v, dslice=dslice: v.tensor_copy(
                                    dslice, exb[:]),
                                  reads=["RI1a"], writes=[dkey])
                            elif kelu == 33:
                                O("scalar", lambda e, p=p: e.activation(
                                    ryb[:], p[:], AF.Relu),
                                  reads=[pk], writes=["RI2a"])
                                O("vector", lambda v, p=p, dslice=dslice: v.tensor_copy(
                                    dslice, p[:]),
                                  reads=[pk], writes=[dkey])
                            elif kelu == 34:
                                O("scalar", lambda e, p=p, dslice=dslice: e.activation(
                                    dslice, p[:], AF.Relu),
                                  reads=[pk], writes=[dkey])
                        elif kind == "elu":
                            if kelu & 1:
                                O("vector", lambda v, p=p, bcol=bcol: v.tensor_scalar(
                                    exb[:], p[:], bcol, 0.0, AL.add, AL.min),
                                  reads=[pk, "bqk"], writes=["RI1a"])
                            else:
                                O("vector", lambda v, p=p: v.tensor_copy(
                                    exb[:], p[:]),
                                  reads=[pk], writes=["RI1a"])
                            if kelu & 2:
                                O("scalar", lambda e, p=p, bcol=bcol: e.activation(
                                    ryb[:], p[:], AF.Relu, bias=bcol, scale=1.0),
                                  reads=[pk, "bqk"], writes=["RI2a"])
                            else:
                                O("scalar", lambda e, p=p: e.activation(
                                    ryb[:], p[:], AF.Relu),
                                  reads=[pk], writes=["RI2a"])
                            if kelu & 4:
                                O("scalar", lambda e: e.activation(
                                    exb[:], exb[:], AF.Exp),
                                  reads=["RI1a"], writes=["RI1a"])
                            if kelu & 8:
                                O("vector", lambda v, dslice=dslice:
                                  v.scalar_tensor_tensor(
                                      dslice, exb[:], -1.0, ryb[:], AL.add, AL.add),
                                  reads=["RI1a", "RI2a"], writes=[dkey])
                            else:
                                O("vector", lambda v, dslice=dslice: v.tensor_copy(
                                    dslice, exb[:]),
                                  reads=["RI1a"], writes=[dkey])
                        else:
                            O("scalar", lambda e, p=p, bcol=bcol, dslice=dslice:
                              e.activation(dslice, p[:], AF.Silu,
                                           bias=bcol, scale=1.0),
                              reads=[pk, "bv"], writes=[dkey])

            if own:
                for eh in range(16):
                    O("tensor", lambda e, eh=eh: e.matmul(
                        ps[4][:], r1t_sb[:, eh * 128:(eh + 1) * 128],
                        rc32[:, eh * 512 + 0:eh * 512 + 512],
                        start=(eh == 0), stop=(eh == 15)),
                      reads=["r1t", "RC"], writes=[PSK[4]])
                O("scalar", lambda e: e.activation(sl1[:], ps[4][:], AF.Silu),
                  reads=[PSK[4]], writes=["sl1"])
                O("tensor", lambda e: e.matmul(
                    ps[5][0:64, :], r2t_sb[:], sl1[:], start=True, stop=True),
                  reads=["r2t", "sl1"], writes=[PSK[5]])
                O("vector", lambda v: v.tensor_copy(sl2[:], ps[5][0:64, :]),
                  reads=[PSK[5]], writes=["sl2"])
                for g in range(4):
                    O("tensor", lambda e, g=g: e.transpose(
                        ps[4][:, g * 64:(g + 1) * 64],
                        sl2[0:64, g * 128:(g + 1) * 128], idf[0:64, 0:64]),
                      reads=["sl2", "idf"], writes=[PSK[4]])
                O("vector", lambda v: v.tensor_copy(lg4[:], ps[4][:, 0:256]),
                  reads=[PSK[4]], writes=["lg4"])
                for g in range(4):
                    lg = lg4[:, g * 64:(g + 1) * 64]
                    O("vector", lambda v, lg=lg: v.max(out=m8[:], in_=lg),
                      reads=["lg4"], writes=["m8"])
                    O("vector", lambda v, lg=lg: v.match_replace(
                        out=wkb[:], in_to_replace=m8[:], in_values=lg,
                        imm_value=-1e30),
                      reads=["lg4", "m8"], writes=["wkb", "m8"])
                    O("vector", lambda v: v.max(out=m8[:], in_=wkb[:]),
                      reads=["wkb"], writes=["m8"])
                    O("vector", lambda v: v.match_replace(
                        out=wk2[:], in_to_replace=m8[:], in_values=wkb[:],
                        imm_value=-1e30),
                      reads=["wkb", "m8"], writes=["wk2", "m8"])
                    O("vector", lambda v, lg=lg: v.tensor_tensor(
                        mtok[:], lg, wk2[:], AL.is_gt),
                      reads=["lg4", "wk2"], writes=["mtok"])
                    O("tensor", lambda e, g=g: e.transpose(
                        pb[0][0:64, g * 128:(g + 1) * 128], mtok[:], idb[:]),
                      reads=["mtok", "idb"], writes=[PBK[0]])
                O("vector", lambda v, tt=tt: v.tensor_copy(
                    mask_fm[0:64, tt * 512:(tt + 1) * 512], pb[0][0:64, 0:512]),
                  reads=[PBK[0]], writes=["maskfm"])

            if astage < 4:
                return
            if own:
                dstq = QSP[:, tt * 512:(tt + 1) * 512].rearrange(
                    "(fh fl) n -> fl fh n", fl=128)
                qv = qfm[:].rearrange("fl (fh n) -> fl fh n", fh=16)
                O("sync", lambda e, dstq=dstq, qv=qv: e.dma_start(
                    out=dstq, in_=qv),
                  reads=["RD1"], writes=["spq"], dma="spq")

            if astage < 5:
                return
            for g in range(4):
                for (src, dst, skey, dkey) in ((qfm, qtok, "RD1", "RI1"),
                                               (vfm, vtok, "RD2", "RI2")):
                    for half in range(2):
                        pbt = pb[half]
                        pbk = PBK[half]
                        for j in range(8):
                            fh = half * 8 + j
                            sl = src[:, fh * 512 + g * 128:
                                     fh * 512 + (g + 1) * 128]
                            O("tensor", lambda e, pbt=pbt, j=j, sl=sl:
                              e.transpose(pbt[:, j * 128:(j + 1) * 128],
                                          sl, idb[:]),
                              reads=[skey, "idb"], writes=[pbk])
                        O("vector", lambda v, pbt=pbt, dst=dst, half=half:
                          v.tensor_copy(
                              dst[:, half * 1024:(half + 1) * 1024], pbt[:]),
                          reads=[pbk], writes=QTKEYS if dst is qtok else VTKEYS)
                for h in range(16):
                    j = h // 4
                    O("tensor", lambda e, h=h, j=j: e.matmul(
                        ps[2 + j][:, (h % 4) * 128:(h % 4 + 1) * 128],
                        qtok[:, h * 128:(h + 1) * 128],
                        vtok[:, h * 128:(h + 1) * 128],
                        start=True, stop=True),
                      reads=QTKEYS + VTKEYS, writes=[PSK[2 + j]])
                for j in range(4):
                    O("vector", lambda v, j=j: v.tensor_tensor(
                        kvacc[:, j * 512:(j + 1) * 512],
                        kvacc[:, j * 512:(j + 1) * 512],
                        ps[2 + j][:], AL.add),
                      reads=KVKEYS + [PSK[2 + j]], writes=KVKEYS)


        a_load(0)
        a_ln(0)
        for tt in range(na):
            a_tr(tt)
            if tt + 1 < na:
                a_load(tt + 1)
                a_ln(tt + 1)
            a_rest(tt)

        O("sync", lambda e: e.dma_start(out=KVIB.ap(), in_=kvacc[:]),
          reads=KVKEYS, writes=["kvib"], dma="kvx")
        O("gpsimd", lambda g: g.collective_compute(
            "AllReduce", mybir.AluOpType.add,
            replica_groups=[[0, 1], [2, 3], [4, 5], [6, 7]],
            ins=[KVIB.ap().opt()], outs=[KVOB.ap().opt()]),
          reads=["kvib"], writes=["kvob"])
        O("gpsimd", lambda g: g.dma_start(out=kv_bf[:], in_=KVOB.ap()),
          reads=["kvob"], writes=["kvbf"], dma="kvr")

        for tn in range(nb):
            nto = tn * 512
            src = XNSP[:, nto:nto + 512].rearrange("(eh el) n -> el eh n", el=128)
            xnv = xnfm[:].rearrange("el (eh n) -> el eh n", eh=16)
            O("sync", lambda e, src=src, xnv=xnv: e.dma_start(out=xnv, in_=src),
              reads=["spx"], writes=["RE"], dma="rxn")
            srcq = QSP[:, nto:nto + 512].rearrange("(fh fl) n -> fl fh n", fl=128)
            qtv = qt[:].rearrange("fl (fh n) -> fl fh n", fh=16)
            O("sync", lambda e, srcq=srcq, qtv=qtv: e.dma_start(out=qtv, in_=srcq),
              reads=["spq"], writes=["RC"], dma="rq")

            for hh in range(16):
                p = ps[hh % 4]
                pk = PSK[hh % 4]
                O("tensor", lambda e, p=p, hh=hh: e.matmul(
                    p[:], kv_bf[:, hh * 128:(hh + 1) * 128],
                    qt[:, hh * 512:(hh + 1) * 512], start=True, stop=True),
                  reads=["kvbf", "RC"], writes=[pk])
                O("vector", lambda v, p=p, hh=hh: v.tensor_copy(
                    aos[:, hh * 512:(hh + 1) * 512], p[:]),
                  reads=[pk], writes=["RA"])
                O("scalar", lambda e, p=p, hh=hh: e.activation(
                    aobf[:, hh * 512:(hh + 1) * 512], p[:], AF.Identity),
                  reads=[pk], writes=["RD1"])

            if bstage < 2:
                continue
            for eh in range(16):
                O("tensor", lambda e, eh=eh: e.matmul(
                    ps[2][0:1, :], onec[:],
                    aobf[:, eh * 512:(eh + 1) * 512],
                    start=(eh == 0), stop=(eh == 15)),
                  reads=["onec", "RD1"], writes=[PSK[2]])
            O("vector", lambda v: v.tensor_scalar_mul(
                murow[0:1, 0:512], ps[2][0:1, :], inv_e),
              reads=[PSK[2]], writes=["murow"])
            O("tensor", lambda e: e.matmul(
                ps[4][:], onerf[:], murow[0:1, 0:512], start=True, stop=True),
              reads=["onerf", "murow"], writes=[PSK[4]])
            ucur4 = (ucur, ucur_b, och[0], och[1])
            uck4 = ("RKu", "RKg", "RKo0", "RKo1")

            def u_group(f):
                uc, f1 = f // 2, f % 2
                ring = (wr0a, wr0b)[uc % 2]
                rkey = ("W0A", "W0B")[uc % 2]
                rfam = ("w0a", "w0b")[uc % 2]
                if f1 == 0:
                    wsrc = WU[uc]
                    O("gpsimd", lambda e, wsrc=wsrc, ring=ring: e.dma_start(
                        out=ring[:], in_=wsrc),
                      writes=[rkey], dma=rfam)
                p = ps[f % 4]
                pk = PSK[f % 4]
                for eh in range(16):
                    lhs = ring[:, eh * 256 + f1 * 128:eh * 256 + (f1 + 1) * 128]
                    O("tensor", lambda e, p=p, lhs=lhs, eh=eh: e.matmul(
                        p[:], lhs, xnfm[:, eh * 512:(eh + 1) * 512],
                        start=(eh == 0), stop=(eh == 15)),
                      reads=[rkey, "RE"], writes=[pk])
                O("scalar", lambda e, p=p, f=f: e.activation(
                    ucur4[f % 4][:], p[:], AF.Silu,
                    bias=bu_c[:, f:f + 1], scale=1.0),
                  reads=[pk, "bu"], writes=[uck4[f % 4]])

            def y_group(f):
                tc = (t3, t3b)[f % 2]
                tck = ("RI2a", "sl1")[f % 2]
                O("vector", lambda v, f=f, tc=tc: v.tensor_tensor(
                    tc[:], aos[:, f * 512:(f + 1) * 512], ps[4][:],
                    AL.subtract),
                  reads=["RA", PSK[4]], writes=[tck])
                O("vector", lambda v, tc=tc: v.tensor_tensor(
                    tc[:], tc[:], ps[5][:], AL.mult),
                  reads=[tck, PSK[5]], writes=[tck])
                O("scalar", lambda e, f=f, tc=tc: e.activation(
                    tc[:], tc[:], AF.Identity,
                    bias=lnb_c[:, f:f + 1], scale=lng_c[:, f:f + 1]),
                  reads=[tck, "lng", "lnb"], writes=[tck])
                O("vector", lambda v, f=f, tc=tc: v.tensor_tensor(
                    ybuf[:, f * 512:(f + 1) * 512], tc[:], ucur4[f % 4][:],
                    AL.mult),
                  reads=[tck, uck4[f % 4]], writes=["RD2"])

            for f in range(4):
                u_group(f)
            for eh in range(16):
                sq = sqb[eh % 2]
                sqk = ("RI2b0", "RI2b1")[eh % 2]
                O("vector", lambda v, eh=eh, sq=sq: v.tensor_tensor(
                    sq[:], aos[:, eh * 512:(eh + 1) * 512], ps[4][:],
                    AL.subtract),
                  reads=["RA", PSK[4]], writes=[sqk])
                O("scalar", lambda e, sq=sq: e.activation(
                    sq[:], sq[:], AF.Square),
                  reads=[sqk], writes=[sqk])
                O("tensor", lambda e, sq=sq, eh=eh: e.matmul(
                    ps[3][0:1, :], onec[:], sq[:],
                    start=(eh == 0), stop=(eh == 15)),
                  reads=["onec", sqk], writes=[PSK[3]])
            O("vector", lambda v: v.tensor_scalar(
                murow[0:1, 512:1024], ps[3][0:1, :], inv_e, eps,
                AL.mult, AL.add),
              reads=[PSK[3]], writes=["murow"])
            O("vector", lambda v: v.reciprocal(
                murow[0:1, 512:1024], murow[0:1, 512:1024]),
              reads=["murow"], writes=["murow"])
            O("scalar", lambda e: e.activation(
                murow[0:1, 512:1024], murow[0:1, 512:1024], AF.Sqrt),
              reads=["murow"], writes=["murow"])
            O("tensor", lambda e: e.matmul(
                ps[5][:], onerf[:], murow[0:1, 512:1024], start=True, stop=True),
              reads=["onerf", "murow"], writes=[PSK[5]])
            for f in range(4):
                y_group(f)
            for f in range(4, 16):
                u_group(f)
                y_group(f)
            def ffn_chunks(ic_lo, ic_hi):
              for ic in range(ic_lo, ic_hi):
                  gring = (wr0a, wr0b)[ic % 2]
                  gkey = ("W0A", "W0B")[ic % 2]
                  gfam = ("w0a", "w0b")[ic % 2]
                  uring = (wr1a, wr1b)[ic % 2]
                  ukey = ("W1A", "W1B")[ic % 2]
                  ufam = ("w1a", "w1b")[ic % 2]
                  g_src = GT[ic]
                  u_src = UT[ic]
                  O("gpsimd", lambda e, g_src=g_src, gring=gring: e.dma_start(
                      out=gring[:], in_=g_src),
                    writes=[gkey], dma=gfam)
                  O("gpsimd", lambda e, u_src=u_src, uring=uring: e.dma_start(
                      out=uring[:], in_=u_src),
                    writes=[ukey], dma=ufam)
                  for isub in range(2):
                      blk = ic * 2 + isub
                      pg = ps[blk % 2]
                      pgk = PSK[blk % 2]
                      pu = ps[2 + blk % 2]
                      puk = PSK[2 + blk % 2]
                      pm = ps[4 + blk % 2]
                      pmk = PSK[4 + blk % 2]
                      for eh in range(16):
                          lhs = gring[:, eh * 256 + isub * 128:
                                      eh * 256 + (isub + 1) * 128]
                          O("tensor", lambda e, pg=pg, lhs=lhs, eh=eh: e.matmul(
                              pg[:], lhs, xnfm[:, eh * 512:(eh + 1) * 512],
                              start=(eh == 0), stop=(eh == 15)),
                            reads=[gkey, "RE"], writes=[pgk])
                      for eh in range(16):
                          lhs = uring[:, eh * 256 + isub * 128:
                                      eh * 256 + (isub + 1) * 128]
                          O("tensor", lambda e, pu=pu, lhs=lhs, eh=eh: e.matmul(
                              pu[:], lhs, xnfm[:, eh * 512:(eh + 1) * 512],
                              start=(eh == 0), stop=(eh == 15)),
                            reads=[ukey, "RE"], writes=[puk])
                      O("tensor", lambda e, pm=pm, blk=blk, tn=tn: e.matmul(
                          pm[:], oh_sb[0:64, blk * 128:(blk + 1) * 128],
                          mask_fm[0:64, tn * 512:(tn + 1) * 512],
                          start=True, stop=True),
                        reads=["oh", "maskfm"], writes=[pmk])
                      O("scalar", lambda e, pg=pg: e.activation(
                          gch[:], pg[:], AF.Silu),
                        reads=[pgk], writes=["RKg"])
                      O("vector", lambda v, pu=pu: v.tensor_tensor(
                          t3[:], gch[:], pu[:], AL.mult),
                        reads=["RKg", puk], writes=["RI2a"])
                      hdst = hid[blk // 16][:, (blk % 16) * 512:
                                            (blk % 16 + 1) * 512]
                      hkey = ("RC", "RC", "RD1", "RD2")[blk // 16]
                      O("vector", lambda v, pm=pm, hdst=hdst: v.tensor_tensor(
                          hdst, t3[:], pm[:], AL.mult),
                        reads=["RI2a", pmk], writes=[hkey])

            if bstage < 6:
                  continue

            ffn_chunks(0, 16)
            for ec in range(8):
                ring = (wr0a, wr0b)[ec % 2]
                rkey = ("W0A", "W0B")[ec % 2]
                rfam = ("w0a", "w0b")[ec % 2]
                wsrc = WO[ec]
                O("gpsimd", lambda e, wsrc=wsrc, ring=ring: e.dma_start(
                    out=ring[:], in_=wsrc),
                  writes=[rkey], dma=rfam)
                for e1 in range(2):
                    ei = ec * 2 + e1
                    p = ps[ei % 4]
                    pk = PSK[ei % 4]
                    for hh in range(16):
                        lhs = ring[:, hh * 256 + e1 * 128:hh * 256 + (e1 + 1) * 128]
                        O("tensor", lambda e, p=p, lhs=lhs, hh=hh: e.matmul(
                            p[:], lhs, ybuf[:, hh * 512:(hh + 1) * 512],
                            start=(hh == 0), stop=(hh == 15)),
                          reads=[rkey, "RD2"], writes=[pk])
                    O("scalar", lambda e, p=p, ei=ei: e.activation(
                        aos[:, ei * 512:(ei + 1) * 512], p[:], AF.Identity,
                        bias=bo_c[:, ei:ei + 1], scale=1.0),
                      reads=[pk, "bo"], writes=["RA"])

            if bstage < 5:
                continue
            ffn_chunks(16, 32)

            for d in range(16):
                dring = (wr0, wr1)[d % 2]
                dkeys = (["W0A", "W0B"], ["W1A", "W1B"])[d % 2]
                dfam = ("dna", "dnb")[d % 2]
                dsrc = DT[d]
                O("gpsimd", lambda e, dsrc=dsrc, dring=dring: e.dma_start(
                    out=dring[:], in_=dsrc),
                  writes=dkeys, dma=dfam)
                xsrc = XT[d * 128:(d + 1) * 128, nto:nto + 512]
                xj = (xtj0, xtj1)[d % 2]
                O("sync", lambda e, xsrc=xsrc, xj=xj: e.dma_start(
                    out=xj[:], in_=xsrc),
                  writes=[("RI1a", "RI1b")[d % 2]], dma=("xtj0", "xtj1")[d % 2])
                p = ps[d % 4]
                pk = PSK[d % 4]
                for ih in range(64):
                    hsl = hid[ih // 16][:, (ih % 16) * 512:(ih % 16 + 1) * 512]
                    hkey = ("RC", "RC", "RD1", "RD2")[ih // 16]
                    O("tensor", lambda e, p=p, ih=ih, hsl=hsl, dring=dring:
                      e.matmul(p[:], dring[:, ih * 128:(ih + 1) * 128], hsl,
                               start=(ih == 0), stop=(ih == 63)),
                      reads=dkeys + [hkey], writes=[pk])
                O("scalar", lambda e, p=p: e.activation(gch[:], p[:], AF.Silu),
                  reads=[pk], writes=["RKg"])
                oc = och[d % 2]
                ok = ("RKo0", "RKo1")[d % 2]
                xjk = ("RI1a", "RI1b")[d % 2]
                O("vector", lambda v, oc=oc, d=d: v.tensor_tensor(
                    oc[:], aos[:, d * 512:(d + 1) * 512], gch[:], AL.mult),
                  reads=["RA", "RKg"], writes=[ok])
                O("vector", lambda v, oc=oc, xj=xj: v.tensor_tensor(
                    oc[:], oc[:], xj[:], AL.add),
                  reads=[ok, xjk], writes=[ok])
                odst = OUT[d * 128:(d + 1) * 128, nto:nto + 512]
                O("sync", lambda e, odst=odst, oc=oc: e.dma_start(
                    out=odst, in_=oc[:]),
                  reads=[ok], writes=[f"outd{d % 2}"], dma=("out0", "out1")[d % 2])

        blk = ctx.enter_context(nc.Block())
        fams = tuple(f for f in ("out0", "out1") if any(o["dma"] == f for o in s.ops))
        s.build(blk, final_dma_fams=fams)

    return nc, s


_CACHE = {}


def make_in_maps(inputs, ncores=8):
    f32 = np.float32
    x = np.ascontiguousarray(np.asarray(inputs["L0_attn_x"], f32))
    def chunkw(w, nch, cw):
        w = np.asarray(w, f32)
        kh = w.shape[0] // 128
        return np.ascontiguousarray(
            w.reshape(kh, 128, nch, cw).transpose(2, 1, 0, 3).reshape(
                nch, 128, kh * cw))

    wqk = chunkw(inputs["L0_attn_Wqk"], 8, 256)
    wv = chunkw(inputs["L0_attn_Wv"], 8, 256)
    wu = chunkw(inputs["L0_attn_Wu"], 8, 256)
    wo = chunkw(inputs["L0_attn_Wo"], 8, 256)
    gateT = chunkw(np.asarray(inputs["L0_ffn_gate_w"], f32).T, 32, 256)
    upT = chunkw(np.asarray(inputs["L0_ffn_up_w"], f32).T, 32, 256)
    downT = chunkw(np.asarray(inputs["L0_ffn_down_w"], f32).T, 16, 128)
    r1T = np.ascontiguousarray(np.asarray(inputs["L0_ffn_router_w1"], f32).T)
    r2T = np.ascontiguousarray(np.asarray(inputs["L0_ffn_router_w2"], f32).T)

    def col(v):
        return np.ascontiguousarray(np.asarray(v, f32).reshape(16, 128).T)

    consts = {
        "WQK": wqk, "WV": wv, "WU": wu, "WO": wo,
        "GT": gateT, "UT": upT, "DT": downT, "R1T": r1T, "R2T": r2T,
        "BQK": col(inputs["L0_attn_bqk"]), "BV": col(inputs["L0_attn_bv"]),
        "BU": col(inputs["L0_attn_bu"]), "BO": col(inputs["L0_attn_bo"]),
        "LNG": col(inputs["L0_attn_ln_g"]), "LNB": col(inputs["L0_attn_ln_b"]),
        "IDB": np.eye(128, dtype=ml_dtypes.bfloat16),
        "IDF": np.eye(128, dtype=f32),
        "ONEC": np.ones((128, 1), dtype=ml_dtypes.bfloat16),
        "ONERF": np.ones((1, 128), dtype=f32),
        "OH": np.kron(np.eye(64), np.ones((1, 128))).astype(ml_dtypes.bfloat16),
    }

    in_maps = []
    for c in range(ncores):
        b, h = c >> 1, c & 1
        own = x[b, h * 2048:(h + 1) * 2048]
        m = dict(consts)
        m["X"] = np.ascontiguousarray(own)
        m["XT"] = np.ascontiguousarray(own.T)
        in_maps.append(m)
    return in_maps


def kernel(**inputs):
    in_maps = make_in_maps(inputs)
    if "nc" not in _CACHE:
        _CACHE["nc"] = build_program()
    nc, _ = _CACHE["nc"]

    res = run_bass_kernel_spmd(nc, in_maps, core_ids=list(range(8)))

    out = np.empty((4, 4096, 2048), np.float32)
    for c, r in enumerate(res.results):
        b, h = c >> 1, c & 1
        out[b, h * 2048:(h + 1) * 2048, :] = r["OUT"].T
    return out


if __name__ == "__main__":
    nc, s = build_program()
    print("ops:", len(s.ops), "incs:", s.n_incs)


# version 18
# speedup vs baseline: 1.1715x; 1.0903x over previous
import numpy as np
import ml_dtypes

import concourse.bass as bass
import concourse.mybir as mybir
from concourse.alu_op_type import AluOpType
from concourse.bass_utils import run_bass_kernel_spmd
from contextlib import ExitStack

F32 = mybir.dt.float32
BF16 = mybir.dt.bfloat16
AF = mybir.ActivationFunctionType
AX = mybir.AxisListType.X
AL = AluOpType

ENGINES = ("tensor", "vector", "scalar", "gpsimd", "sync")
KB = 1024


class Sched:

    def __init__(self, nc, ctx):
        self.nc = nc
        self.ctx = ctx
        self.ops = []

    def op(self, engine, fn, reads=(), writes=(), dma=None):
        self.ops.append(dict(engine=engine, fn=fn, reads=tuple(reads),
                             writes=tuple(writes), dma=dma))

    def build(self, block, final_dma_fams=()):
        nc = self.nc
        n = len(self.ops)
        last_w = {}
        readers = {}
        deps = [None] * n

        def domain(j):
            oj = self.ops[j]
            return ("D", oj["dma"]) if oj["dma"] else ("E", oj["engine"])

        EXCL = {f"ps{i}" for i in range(8)} | {"pb0", "pb1"}
        for i, o in enumerate(self.ops):
            rds = [b for b in o["reads"] if b not in EXCL]
            wrs = list(o["writes"]) + [b for b in o["reads"] if b in EXCL]
            dep = set()
            for b in rds:
                if b in last_w:
                    dep.add(last_w[b])
            for b in wrs:
                if b in last_w:
                    dep.add(last_w[b])
                dep.update(readers.get(b, ()))
            dep.discard(i)
            byd = {}
            for d in dep:
                k = domain(d)
                if k not in byd or d > byd[k]:
                    byd[k] = d
            if o["engine"] == "tensor" and not o["dma"]:
                byd.pop(("E", "tensor"), None)
            deps[i] = sorted(byd.values())
            for b in rds:
                readers.setdefault(b, []).append(i)
            for b in wrs:
                last_w[b] = i
                readers[b] = []

        needed = set()
        for i in range(n):
            for d in deps[i]:
                if not self.ops[d]["dma"]:
                    needed.add(d)

        self.esem = {e: self.ctx.enter_context(nc.semaphore(f"es_{e}"))
                     for e in ENGINES}
        fams = sorted({o["dma"] for o in self.ops if o["dma"]})
        self.dsem = {f: self.ctx.enter_context(nc.semaphore(f"ds_{f}"))
                     for f in fams}
        ecnt = {e: 0 for e in ENGINES}
        dcnt = {f: 0 for f in fams}
        evs = [None] * n
        for i, o in enumerate(self.ops):
            if o["dma"]:
                dcnt[o["dma"]] += 16
                evs[i] = (self.dsem[o["dma"]], dcnt[o["dma"]])
                o["inc"] = True
            elif i in needed:
                ecnt[o["engine"]] += 1
                evs[i] = (self.esem[o["engine"]], ecnt[o["engine"]])
                o["inc"] = True
            else:
                o["inc"] = False
        self.dma_totals = dict(dcnt)
        self.n_incs = len(needed)

        per_engine = {e: [] for e in ENGINES}
        for i, o in enumerate(self.ops):
            per_engine[o["engine"]].append(i)
        sched = self

        def run(eng, engine_name):
            water = {}
            for i in per_engine[engine_name]:
                o = sched.ops[i]
                for d in deps[i]:
                    sem, val = evs[d]
                    k = id(sem)
                    if water.get(k, -1) >= val:
                        continue
                    eng.wait_ge(sem, val)
                    water[k] = val
                inst = o["fn"](eng)
                if o["inc"] and inst is not None:
                    sem, val = evs[i]
                    inst.then_inc(sem, 16 if o["dma"] else 1)
            if engine_name == "sync":
                for f in final_dma_fams:
                    eng.wait_ge(sched.dsem[f], sched.dma_totals[f])

        block.tensor(lambda t: run(t, "tensor"))
        block.vector(lambda v: run(v, "vector"))
        block.scalar(lambda s: run(s, "scalar"))
        block.gpsimd(lambda g: run(g, "gpsimd"))
        block.sync(lambda sy: run(sy, "sync"))


class Overlay:

    def __init__(self, nc, ctx, total_bytes):
        self.nc = nc
        self.back = ctx.enter_context(
            nc.sbuf_tensor("ovl_arena", [128, total_bytes], mybir.dt.uint8))
        self.base = nc.lookup_mloc(self.back).addr
        self.total = total_bytes

    def t(self, name, shape, dtype, off):
        per_part = int(np.prod(shape[1:])) * mybir.dt.size(dtype)
        assert off + per_part <= self.total, (name, off, per_part)
        return self.nc.alloc_sbuf_tensor_at(
            name, list(shape), dtype, offset=self.base + off)


def build_program(na=4, nb=4, bstage=9, astage=9, kproj=9, kelu=15):
    nc = bass.Bass("TRN2", target_bir_lowering=False, debug=False)

    X = nc.dram_tensor("X", [2048, 2048], F32, kind="ExternalInput").ap()
    XT = nc.dram_tensor("XT", [2048, 2048], F32, kind="ExternalInput").ap()
    WQK = nc.dram_tensor("WQK", [8, 128, 4096], F32, kind="ExternalInput").ap()
    WV = nc.dram_tensor("WV", [8, 128, 4096], F32, kind="ExternalInput").ap()
    WU = nc.dram_tensor("WU", [8, 128, 4096], F32, kind="ExternalInput").ap()
    WO = nc.dram_tensor("WO", [8, 128, 4096], F32, kind="ExternalInput").ap()
    GT = nc.dram_tensor("GT", [32, 128, 4096], F32, kind="ExternalInput").ap()
    UT = nc.dram_tensor("UT", [32, 128, 4096], F32, kind="ExternalInput").ap()
    DT = nc.dram_tensor("DT", [16, 128, 8192], F32, kind="ExternalInput").ap()
    R1T = nc.dram_tensor("R1T", [2048, 128], F32, kind="ExternalInput").ap()
    R2T = nc.dram_tensor("R2T", [128, 64], F32, kind="ExternalInput").ap()
    BQK = nc.dram_tensor("BQK", [128, 16], F32, kind="ExternalInput").ap()
    BV = nc.dram_tensor("BV", [128, 16], F32, kind="ExternalInput").ap()
    BU = nc.dram_tensor("BU", [128, 16], F32, kind="ExternalInput").ap()
    BO = nc.dram_tensor("BO", [128, 16], F32, kind="ExternalInput").ap()
    LNG = nc.dram_tensor("LNG", [128, 16], F32, kind="ExternalInput").ap()
    LNB = nc.dram_tensor("LNB", [128, 16], F32, kind="ExternalInput").ap()
    IDB = nc.dram_tensor("IDB", [128, 128], BF16, kind="ExternalInput").ap()
    IDF = nc.dram_tensor("IDF", [128, 128], F32, kind="ExternalInput").ap()
    ONEC = nc.dram_tensor("ONEC", [128, 1], BF16, kind="ExternalInput").ap()
    ONERF = nc.dram_tensor("ONERF", [1, 128], F32, kind="ExternalInput").ap()
    OH = nc.dram_tensor("OH", [64, 8192], BF16, kind="ExternalInput").ap()

    OUT = nc.dram_tensor("OUT", [2048, 2048], F32, kind="ExternalOutput").ap()

    XNSP = nc.dram_tensor("XNSP", [2048, 2048], BF16, kind="Internal").ap()
    KVIB = nc.dram_tensor("KVIB", [128, 2048], F32, kind="Internal")
    KVOB = nc.dram_tensor("KVOB", [128, 2048], F32, kind="Internal")
    DTB = nc.dram_tensor("DTB", [16, 128, 8192], BF16, kind="Internal").ap()
    QSP = nc.dram_tensor("QSP", [2048, 2048], BF16, kind="Internal").ap()

    with ExitStack() as ctx:
        ov = Overlay(nc, ctx, 204 * KB)
        t = ov.t
        oh_sb = t("oh_sb", [64, 8192], BF16, 0 * KB)
        r1t_sb = t("r1t_sb", [128, 2048], F32, 16 * KB)
        mask_fm = t("mask_fm", [64, 2048], BF16, 24 * KB)
        kv_bf = t("kv_bf", [128, 2048], BF16, 28 * KB)
        idf = t("idf", [128, 128], F32, 32 * KB)
        idb = t("idb", [128, 128], BF16, 32 * KB + 512)
        r2t_sb = t("r2t_sb", [128, 64], F32, 32 * KB + 768)
        bqk_c = t("bqk_c", [128, 16], F32, 33 * KB + 0)
        bv_c = t("bv_c", [128, 16], F32, 33 * KB + 64)
        bu_c = t("bu_c", [128, 16], F32, 33 * KB + 128)
        bo_c = t("bo_c", [128, 16], F32, 33 * KB + 192)
        lng_c = t("lng_c", [128, 16], F32, 33 * KB + 256)
        lnb_c = t("lnb_c", [128, 16], F32, 33 * KB + 320)
        onec = t("onec", [128, 1], BF16, 33 * KB + 384)
        onerf = t("onerf", [1, 128], F32, 33 * KB + 512)
        sl1 = t("sl1", [128, 512], F32, 34 * KB)
        t3b = t("t3b", [128, 512], F32, 34 * KB)
        murow = t("murow", [1, 1024], F32, 36 * KB)
        lg4 = t("lg4", [128, 256], F32, 40 * KB)
        sl2 = t("sl2", [64, 512], F32, 41 * KB)
        m8 = t("m8", [128, 8], F32, 43 * KB)
        wkb = t("wkb", [128, 64], F32, 43 * KB + 32)
        wk2 = t("wk2", [128, 64], F32, 43 * KB + 288)
        mtok = t("mtok", [128, 64], BF16, 43 * KB + 544)
        scol = t("scol", [128, 24], F32, 43 * KB + 672)
        xa = t("xa", [128, 8192], F32, 44 * KB)
        aos = t("aos", [128, 8192], F32, 44 * KB)
        rc32 = t("rc32", [128, 8192], F32, 76 * KB)
        qt = t("qt", [128, 8192], BF16, 76 * KB)
        hid0 = t("hid0", [128, 8192], BF16, 76 * KB)
        hid1 = t("hid1", [128, 8192], BF16, 92 * KB)
        xnfm = t("xnfm", [128, 8192], BF16, 108 * KB)
        qfm = t("qfm", [128, 8192], BF16, 124 * KB)
        aobf = t("aobf", [128, 8192], BF16, 124 * KB)
        hid2 = t("hid2", [128, 8192], BF16, 124 * KB)
        vfm = t("vfm", [128, 8192], BF16, 140 * KB)
        ybuf = t("ybuf", [128, 8192], BF16, 140 * KB)
        hid3 = t("hid3", [128, 8192], BF16, 140 * KB)
        wr0 = t("wr0", [128, 8192], BF16, 156 * KB)
        wr0a = t("wr0a", [128, 4096], BF16, 156 * KB)
        wr0b = t("wr0b", [128, 4096], BF16, 164 * KB)
        wr1 = t("wr1", [128, 8192], BF16, 172 * KB)
        wr1a = t("wr1a", [128, 4096], BF16, 172 * KB)
        wr1b = t("wr1b", [128, 4096], BF16, 180 * KB)
        qtok = t("qtok", [128, 2048], BF16, 188 * KB)
        exb = t("exb", [128, 512], F32, 188 * KB)
        xtj0 = t("xtj0", [128, 512], F32, 188 * KB)
        xtj1 = t("xtj1", [128, 512], F32, 190 * KB)
        vtok = t("vtok", [128, 2048], BF16, 192 * KB)
        ryb = t("ryb", [128, 512], F32, 192 * KB)
        t3 = t("t3", [128, 512], F32, 192 * KB)
        sqb = [t("sqb0", [128, 512], BF16, 194 * KB),
               t("sqb1", [128, 512], BF16, 195 * KB)]
        kvacc = t("kvacc", [128, 2048], F32, 196 * KB)
        ucur = t("ucur", [128, 512], F32, 196 * KB)
        gch = t("gch", [128, 512], F32, 198 * KB)
        ucur_b = t("ucur_b", [128, 512], F32, 198 * KB)
        och = [t("och0", [128, 512], F32, 200 * KB),
               t("och1", [128, 512], F32, 202 * KB)]

        hid = (hid0, hid1, hid2, hid3)
        KVKEYS = ["RKu", "RKg", "RKo0", "RKo1"]
        QTKEYS = ["RI1a", "RI1b"]
        VTKEYS = ["RI2a", "RI2b0", "RI2b1"]

        ps = [ctx.enter_context(nc.psum_tensor(f"ps{i}", [128, 512], F32))
              for i in range(6)]
        pb = [ctx.enter_context(nc.psum_tensor(f"pq{i}", [128, 1024], BF16))
              for i in range(2)]
        PSK = [f"ps{i}" for i in range(6)]
        PBK = ["pb0", "pb1"]

        s = Sched(nc, ctx)
        O = s.op

        def cl(dst, src, key, fam):
            O("sync", lambda e, dst=dst, src=src: e.dma_start(out=dst[:], in_=src),
              writes=[key], dma=fam)

        cl(oh_sb, OH, "oh", "c_oh")
        cl(idf, IDF, "idf", "c_id")
        cl(idb, IDB, "idb", "c_id2")
        cl(r2t_sb, R2T, "r2t", "c_r2")
        cl(bqk_c, BQK, "bqk", "c_b1")
        cl(bv_c, BV, "bv", "c_b2")
        cl(bu_c, BU, "bu", "c_b3")
        cl(bo_c, BO, "bo", "c_b4")
        cl(lng_c, LNG, "lng", "c_b5")
        cl(lnb_c, LNB, "lnb", "c_b6")
        cl(onec, ONEC, "onec", "c_o1")
        cl(onerf, ONERF, "onerf", "c_o2")
        O("sync", lambda e: e.dma_start(
            out=r1t_sb[:].rearrange("el (eh r) -> el eh r", eh=16),
            in_=R1T.rearrange("(eh el) r -> el eh r", el=128)),
          writes=["r1t"], dma="c_r1")

        O("vector", lambda v: v.memset(kvacc[:], 0.0), writes=KVKEYS)

        inv_e = float(1.0 / 2048.0)
        eps = float(1e-5)

        for tt in range(na):
            own = tt < 4
            xsrc = X[tt * 512:(tt + 1) * 512, :].rearrange(
                "(g p) e -> p g e", p=128)
            xdst = xa[:].rearrange("p (g e) -> p g e", g=4)
            O("sync", lambda e, xsrc=xsrc, xdst=xdst: e.dma_start(
                out=xdst, in_=xsrc),
              writes=["RA"], dma="xa")

            for g in range(4):
                xg = xa[:, g * 2048:(g + 1) * 2048]
                O("vector", lambda v, xg=xg, g=g: v.reduce_sum(
                    scol[:, g:g + 1], xg, AX),
                  reads=["RA"], writes=["scolA"])
                O("scalar", lambda e, xg=xg, g=g: e.activation(
                    rc32[:, g * 2048:(g + 1) * 2048], xg, AF.Square,
                    accum_out=scol[:, 4 + g:5 + g]),
                  reads=["RA"], writes=["RC", "scolB"])
            O("vector", lambda v: v.tensor_scalar_mul(
                scol[:, 8:12], scol[:, 0:4], inv_e),
              reads=["scolA"], writes=["scolC"])
            O("vector", lambda v: v.tensor_scalar_mul(
                scol[:, 12:16], scol[:, 4:8], inv_e),
              reads=["scolB"], writes=["scolD"])
            O("vector", lambda v: v.tensor_tensor(
                scol[:, 16:20], scol[:, 8:12], scol[:, 8:12], AL.mult),
              reads=["scolC"], writes=["scolE"])
            O("vector", lambda v: v.tensor_sub(
                scol[:, 16:20], scol[:, 12:16], scol[:, 16:20]),
              reads=["scolD", "scolE"], writes=["scolE"])
            O("vector", lambda v: v.tensor_scalar_add(
                scol[:, 16:20], scol[:, 16:20], eps),
              reads=["scolE"], writes=["scolE"])
            O("vector", lambda v: v.reciprocal(
                scol[:, 20:24], scol[:, 16:20]),
              reads=["scolE"], writes=["scolF"])
            O("scalar", lambda e: e.activation(
                scol[:, 20:24], scol[:, 20:24], AF.Sqrt),
              reads=["scolF"], writes=["scolF"])
            for g in range(4):
                xg = xa[:, g * 2048:(g + 1) * 2048]
                O("vector", lambda v, xg=xg, g=g: v.tensor_scalar(
                    xg, xg, scol[:, 8 + g:9 + g], scol[:, 20 + g:21 + g],
                    AL.subtract, AL.mult),
                  reads=["RA", "scolC", "scolF"], writes=["RA"])

            for eh in range(16):
                p = ps[eh % 4]
                pk = PSK[eh % 4]
                for g in range(4):
                    src = xa[:, g * 2048 + eh * 128: g * 2048 + (eh + 1) * 128]
                    O("tensor", lambda e, p=p, g=g, src=src: e.transpose(
                        p[:, g * 128:(g + 1) * 128], src, idf[:]),
                      reads=["RA", "idf"], writes=[pk])
                O("vector", lambda v, p=p, eh=eh: v.tensor_copy(
                    xnfm[:, eh * 512:(eh + 1) * 512], p[:]),
                  reads=[pk], writes=["RE"])
                if own:
                    O("scalar", lambda e, p=p, eh=eh: e.activation(
                        rc32[:, eh * 512:(eh + 1) * 512], p[:], AF.Identity),
                      reads=[pk], writes=["RC"])

            if astage < 3:
                continue
            if own:
                dst = XNSP[:, tt * 512:(tt + 1) * 512].rearrange(
                    "(eh el) n -> el eh n", el=128)
                xnv = xnfm[:].rearrange("el (eh n) -> el eh n", eh=16)
                O("sync", lambda e, dst=dst, xnv=xnv: e.dma_start(
                    out=dst, in_=xnv),
                  reads=["RE"], writes=["spx"], dma="spx")

            projs = [(WQK, bqk_c, "elu", qfm, "RD1"),
                     (WV, bv_c, "silu", vfm, "RD2")][:1 if kproj < 4 else 2]
            for (W, bias_c, kind, dst, dkey) in projs:
                for fc in range(8):
                    ring = (wr0a, wr0b)[fc % 2]
                    rkey = ("W0A", "W0B")[fc % 2]
                    rfam = ("w0a", "w0b")[fc % 2]
                    wsrc = W[fc]
                    O("gpsimd", lambda e, wsrc=wsrc, ring=ring: e.dma_start(
                        out=ring[:], in_=wsrc),
                      writes=[rkey], dma=rfam)
                    for f1 in range(2):
                        if kproj < 2:
                            continue
                        f = fc * 2 + f1
                        p = ps[f % 4]
                        pk = PSK[f % 4]
                        for eh in range(16):
                            lhs = ring[:, eh * 256 + f1 * 128:
                                       eh * 256 + (f1 + 1) * 128]
                            rhs = xnfm[:, eh * 512:(eh + 1) * 512]
                            O("tensor", lambda e, p=p, lhs=lhs, rhs=rhs, eh=eh:
                              e.matmul(p[:], lhs, rhs,
                                       start=(eh == 0), stop=(eh == 15)),
                              reads=[rkey, "RE"], writes=[pk])
                        dslice = dst[:, f * 512:(f + 1) * 512]
                        bcol = bias_c[:, f:f + 1]
                        if kproj < 3:
                            O("vector", lambda v, p=p, dslice=dslice: v.tensor_copy(
                                dslice, p[:]),
                              reads=[pk], writes=[dkey])
                        elif kind == "elu" and kelu >= 32:
                            if kelu == 32:
                                O("vector", lambda v, p=p: v.tensor_copy(exb[:], p[:]),
                                  reads=[pk], writes=["RI1a"])
                                O("vector", lambda v, dslice=dslice: v.tensor_copy(
                                    dslice, exb[:]),
                                  reads=["RI1a"], writes=[dkey])
                            elif kelu == 33:
                                O("scalar", lambda e, p=p: e.activation(
                                    ryb[:], p[:], AF.Relu),
                                  reads=[pk], writes=["RI2a"])
                                O("vector", lambda v, p=p, dslice=dslice: v.tensor_copy(
                                    dslice, p[:]),
                                  reads=[pk], writes=[dkey])
                            elif kelu == 34:
                                O("scalar", lambda e, p=p, dslice=dslice: e.activation(
                                    dslice, p[:], AF.Relu),
                                  reads=[pk], writes=[dkey])
                        elif kind == "elu":
                            if kelu & 1:
                                O("vector", lambda v, p=p, bcol=bcol: v.tensor_scalar(
                                    exb[:], p[:], bcol, 0.0, AL.add, AL.min),
                                  reads=[pk, "bqk"], writes=["RI1a"])
                            else:
                                O("vector", lambda v, p=p: v.tensor_copy(
                                    exb[:], p[:]),
                                  reads=[pk], writes=["RI1a"])
                            if kelu & 2:
                                O("scalar", lambda e, p=p, bcol=bcol: e.activation(
                                    ryb[:], p[:], AF.Relu, bias=bcol, scale=1.0),
                                  reads=[pk, "bqk"], writes=["RI2a"])
                            else:
                                O("scalar", lambda e, p=p: e.activation(
                                    ryb[:], p[:], AF.Relu),
                                  reads=[pk], writes=["RI2a"])
                            if kelu & 4:
                                O("scalar", lambda e: e.activation(
                                    exb[:], exb[:], AF.Exp),
                                  reads=["RI1a"], writes=["RI1a"])
                            if kelu & 8:
                                O("vector", lambda v, dslice=dslice:
                                  v.scalar_tensor_tensor(
                                      dslice, exb[:], -1.0, ryb[:], AL.add, AL.add),
                                  reads=["RI1a", "RI2a"], writes=[dkey])
                            else:
                                O("vector", lambda v, dslice=dslice: v.tensor_copy(
                                    dslice, exb[:]),
                                  reads=["RI1a"], writes=[dkey])
                        else:
                            O("scalar", lambda e, p=p, bcol=bcol, dslice=dslice:
                              e.activation(dslice, p[:], AF.Silu,
                                           bias=bcol, scale=1.0),
                              reads=[pk, "bv"], writes=[dkey])

            if own:
                for eh in range(16):
                    O("tensor", lambda e, eh=eh: e.matmul(
                        ps[4][:], r1t_sb[:, eh * 128:(eh + 1) * 128],
                        rc32[:, eh * 512 + 0:eh * 512 + 512],
                        start=(eh == 0), stop=(eh == 15)),
                      reads=["r1t", "RC"], writes=[PSK[4]])
                O("scalar", lambda e: e.activation(sl1[:], ps[4][:], AF.Silu),
                  reads=[PSK[4]], writes=["sl1"])
                O("tensor", lambda e: e.matmul(
                    ps[5][0:64, :], r2t_sb[:], sl1[:], start=True, stop=True),
                  reads=["r2t", "sl1"], writes=[PSK[5]])
                O("vector", lambda v: v.tensor_copy(sl2[:], ps[5][0:64, :]),
                  reads=[PSK[5]], writes=["sl2"])
                for g in range(4):
                    O("tensor", lambda e, g=g: e.transpose(
                        ps[4][:, g * 64:(g + 1) * 64],
                        sl2[0:64, g * 128:(g + 1) * 128], idf[0:64, 0:64]),
                      reads=["sl2", "idf"], writes=[PSK[4]])
                O("vector", lambda v: v.tensor_copy(lg4[:], ps[4][:, 0:256]),
                  reads=[PSK[4]], writes=["lg4"])
                for g in range(4):
                    lg = lg4[:, g * 64:(g + 1) * 64]
                    O("vector", lambda v, lg=lg: v.max(out=m8[:], in_=lg),
                      reads=["lg4"], writes=["m8"])
                    O("vector", lambda v, lg=lg: v.match_replace(
                        out=wkb[:], in_to_replace=m8[:], in_values=lg,
                        imm_value=-1e30),
                      reads=["lg4", "m8"], writes=["wkb", "m8"])
                    O("vector", lambda v: v.max(out=m8[:], in_=wkb[:]),
                      reads=["wkb"], writes=["m8"])
                    O("vector", lambda v: v.match_replace(
                        out=wk2[:], in_to_replace=m8[:], in_values=wkb[:],
                        imm_value=-1e30),
                      reads=["wkb", "m8"], writes=["wk2", "m8"])
                    O("vector", lambda v, lg=lg: v.tensor_tensor(
                        mtok[:], lg, wk2[:], AL.is_gt),
                      reads=["lg4", "wk2"], writes=["mtok"])
                    O("tensor", lambda e, g=g: e.transpose(
                        pb[0][0:64, g * 128:(g + 1) * 128], mtok[:], idb[:]),
                      reads=["mtok", "idb"], writes=[PBK[0]])
                O("vector", lambda v, tt=tt: v.tensor_copy(
                    mask_fm[0:64, tt * 512:(tt + 1) * 512], pb[0][0:64, 0:512]),
                  reads=[PBK[0]], writes=["maskfm"])

            if astage < 4:
                continue
            if own:
                dstq = QSP[:, tt * 512:(tt + 1) * 512].rearrange(
                    "(fh fl) n -> fl fh n", fl=128)
                qv = qfm[:].rearrange("fl (fh n) -> fl fh n", fh=16)
                O("sync", lambda e, dstq=dstq, qv=qv: e.dma_start(
                    out=dstq, in_=qv),
                  reads=["RD1"], writes=["spq"], dma="spq")

            if astage < 5:
                continue
            for g in range(4):
                for (src, dst, skey, dkey) in ((qfm, qtok, "RD1", "RI1"),
                                               (vfm, vtok, "RD2", "RI2")):
                    for half in range(2):
                        pbt = pb[half]
                        pbk = PBK[half]
                        for j in range(8):
                            fh = half * 8 + j
                            sl = src[:, fh * 512 + g * 128:
                                     fh * 512 + (g + 1) * 128]
                            O("tensor", lambda e, pbt=pbt, j=j, sl=sl:
                              e.transpose(pbt[:, j * 128:(j + 1) * 128],
                                          sl, idb[:]),
                              reads=[skey, "idb"], writes=[pbk])
                        O("vector", lambda v, pbt=pbt, dst=dst, half=half:
                          v.tensor_copy(
                              dst[:, half * 1024:(half + 1) * 1024], pbt[:]),
                          reads=[pbk], writes=QTKEYS if dst is qtok else VTKEYS)
                for h in range(16):
                    j = h // 4
                    O("tensor", lambda e, h=h, j=j: e.matmul(
                        ps[2 + j][:, (h % 4) * 128:(h % 4 + 1) * 128],
                        qtok[:, h * 128:(h + 1) * 128],
                        vtok[:, h * 128:(h + 1) * 128],
                        start=True, stop=True),
                      reads=QTKEYS + VTKEYS, writes=[PSK[2 + j]])
                for j in range(4):
                    O("vector", lambda v, j=j: v.tensor_tensor(
                        kvacc[:, j * 512:(j + 1) * 512],
                        kvacc[:, j * 512:(j + 1) * 512],
                        ps[2 + j][:], AL.add),
                      reads=KVKEYS + [PSK[2 + j]], writes=KVKEYS)
            for j in range(4):
                dci = tt * 4 + j
                O("gpsimd", lambda e, dci=dci: e.dma_start(
                    out=DTB[dci], in_=DT[dci]),
                  writes=[f"tdt{dci}"], dma="ppd%d" % dci)

        O("sync", lambda e: e.dma_start(out=KVIB.ap(), in_=kvacc[:]),
          reads=KVKEYS, writes=["kvib"], dma="kvx")
        O("gpsimd", lambda g: g.collective_compute(
            "AllReduce", mybir.AluOpType.add,
            replica_groups=[[0, 1], [2, 3], [4, 5], [6, 7]],
            ins=[KVIB.ap().opt()], outs=[KVOB.ap().opt()]),
          reads=["kvib"], writes=["kvob"])
        O("gpsimd", lambda g: g.dma_start(out=kv_bf[:], in_=KVOB.ap()),
          reads=["kvob"], writes=["kvbf"], dma="kvr")

        for tn in range(nb):
            nto = tn * 512
            src = XNSP[:, nto:nto + 512].rearrange("(eh el) n -> el eh n", el=128)
            xnv = xnfm[:].rearrange("el (eh n) -> el eh n", eh=16)
            O("sync", lambda e, src=src, xnv=xnv: e.dma_start(out=xnv, in_=src),
              reads=["spx"], writes=["RE"], dma="rxn")
            srcq = QSP[:, nto:nto + 512].rearrange("(fh fl) n -> fl fh n", fl=128)
            qtv = qt[:].rearrange("fl (fh n) -> fl fh n", fh=16)
            O("sync", lambda e, srcq=srcq, qtv=qtv: e.dma_start(out=qtv, in_=srcq),
              reads=["spq"], writes=["RC"], dma="rq")

            for hh in range(16):
                p = ps[hh % 4]
                pk = PSK[hh % 4]
                O("tensor", lambda e, p=p, hh=hh: e.matmul(
                    p[:], kv_bf[:, hh * 128:(hh + 1) * 128],
                    qt[:, hh * 512:(hh + 1) * 512], start=True, stop=True),
                  reads=["kvbf", "RC"], writes=[pk])
                O("vector", lambda v, p=p, hh=hh: v.tensor_copy(
                    aos[:, hh * 512:(hh + 1) * 512], p[:]),
                  reads=[pk], writes=["RA"])
                O("scalar", lambda e, p=p, hh=hh: e.activation(
                    aobf[:, hh * 512:(hh + 1) * 512], p[:], AF.Identity),
                  reads=[pk], writes=["RD1"])

            if bstage < 2:
                continue
            for eh in range(16):
                O("tensor", lambda e, eh=eh: e.matmul(
                    ps[2][0:1, :], onec[:],
                    aobf[:, eh * 512:(eh + 1) * 512],
                    start=(eh == 0), stop=(eh == 15)),
                  reads=["onec", "RD1"], writes=[PSK[2]])
            O("vector", lambda v: v.tensor_scalar_mul(
                murow[0:1, 0:512], ps[2][0:1, :], inv_e),
              reads=[PSK[2]], writes=["murow"])
            O("tensor", lambda e: e.matmul(
                ps[4][:], onerf[:], murow[0:1, 0:512], start=True, stop=True),
              reads=["onerf", "murow"], writes=[PSK[4]])
            ucur4 = (ucur, ucur_b, och[0], och[1])
            uck4 = ("RKu", "RKg", "RKo0", "RKo1")

            def u_group(f):
                uc, f1 = f // 2, f % 2
                ring = (wr0a, wr0b)[uc % 2]
                rkey = ("W0A", "W0B")[uc % 2]
                rfam = ("w0a", "w0b")[uc % 2]
                if f1 == 0:
                    wsrc = WU[uc]
                    O("gpsimd", lambda e, wsrc=wsrc, ring=ring: e.dma_start(
                        out=ring[:], in_=wsrc),
                      writes=[rkey], dma=rfam)
                p = ps[f % 4]
                pk = PSK[f % 4]
                for eh in range(16):
                    lhs = ring[:, eh * 256 + f1 * 128:eh * 256 + (f1 + 1) * 128]
                    O("tensor", lambda e, p=p, lhs=lhs, eh=eh: e.matmul(
                        p[:], lhs, xnfm[:, eh * 512:(eh + 1) * 512],
                        start=(eh == 0), stop=(eh == 15)),
                      reads=[rkey, "RE"], writes=[pk])
                O("scalar", lambda e, p=p, f=f: e.activation(
                    ucur4[f % 4][:], p[:], AF.Silu,
                    bias=bu_c[:, f:f + 1], scale=1.0),
                  reads=[pk, "bu"], writes=[uck4[f % 4]])

            def y_group(f):
                tc = (t3, t3b)[f % 2]
                tck = ("RI2a", "sl1")[f % 2]
                O("vector", lambda v, f=f, tc=tc: v.tensor_tensor(
                    tc[:], aos[:, f * 512:(f + 1) * 512], ps[4][:],
                    AL.subtract),
                  reads=["RA", PSK[4]], writes=[tck])
                O("vector", lambda v, tc=tc: v.tensor_tensor(
                    tc[:], tc[:], ps[5][:], AL.mult),
                  reads=[tck, PSK[5]], writes=[tck])
                O("scalar", lambda e, f=f, tc=tc: e.activation(
                    tc[:], tc[:], AF.Identity,
                    bias=lnb_c[:, f:f + 1], scale=lng_c[:, f:f + 1]),
                  reads=[tck, "lng", "lnb"], writes=[tck])
                O("vector", lambda v, f=f, tc=tc: v.tensor_tensor(
                    ybuf[:, f * 512:(f + 1) * 512], tc[:], ucur4[f % 4][:],
                    AL.mult),
                  reads=[tck, uck4[f % 4]], writes=["RD2"])

            for f in range(4):
                u_group(f)
            for eh in range(16):
                sq = sqb[eh % 2]
                sqk = ("RI2b0", "RI2b1")[eh % 2]
                O("vector", lambda v, eh=eh, sq=sq: v.tensor_tensor(
                    sq[:], aos[:, eh * 512:(eh + 1) * 512], ps[4][:],
                    AL.subtract),
                  reads=["RA", PSK[4]], writes=[sqk])
                O("scalar", lambda e, sq=sq: e.activation(
                    sq[:], sq[:], AF.Square),
                  reads=[sqk], writes=[sqk])
                O("tensor", lambda e, sq=sq, eh=eh: e.matmul(
                    ps[3][0:1, :], onec[:], sq[:],
                    start=(eh == 0), stop=(eh == 15)),
                  reads=["onec", sqk], writes=[PSK[3]])
            O("vector", lambda v: v.tensor_scalar(
                murow[0:1, 512:1024], ps[3][0:1, :], inv_e, eps,
                AL.mult, AL.add),
              reads=[PSK[3]], writes=["murow"])
            O("vector", lambda v: v.reciprocal(
                murow[0:1, 512:1024], murow[0:1, 512:1024]),
              reads=["murow"], writes=["murow"])
            O("scalar", lambda e: e.activation(
                murow[0:1, 512:1024], murow[0:1, 512:1024], AF.Sqrt),
              reads=["murow"], writes=["murow"])
            O("tensor", lambda e: e.matmul(
                ps[5][:], onerf[:], murow[0:1, 512:1024], start=True, stop=True),
              reads=["onerf", "murow"], writes=[PSK[5]])
            for f in range(4):
                y_group(f)
            for f in range(4, 16):
                u_group(f)
                y_group(f)
            def ffn_chunks(ic_lo, ic_hi):
              for ic in range(ic_lo, ic_hi):
                  gring = (wr0a, wr0b)[ic % 2]
                  gkey = ("W0A", "W0B")[ic % 2]
                  gfam = ("w0a", "w0b")[ic % 2]
                  uring = (wr1a, wr1b)[ic % 2]
                  ukey = ("W1A", "W1B")[ic % 2]
                  ufam = ("w1a", "w1b")[ic % 2]
                  g_src = GT[ic]
                  u_src = UT[ic]
                  O("gpsimd", lambda e, g_src=g_src, gring=gring: e.dma_start(
                      out=gring[:], in_=g_src),
                    writes=[gkey], dma=gfam)
                  O("gpsimd", lambda e, u_src=u_src, uring=uring: e.dma_start(
                      out=uring[:], in_=u_src),
                    writes=[ukey], dma=ufam)
                  for isub in range(2):
                      blk = ic * 2 + isub
                      pg = ps[blk % 2]
                      pgk = PSK[blk % 2]
                      pu = ps[2 + blk % 2]
                      puk = PSK[2 + blk % 2]
                      pm = ps[4 + blk % 2]
                      pmk = PSK[4 + blk % 2]
                      for eh in range(16):
                          lhs = gring[:, eh * 256 + isub * 128:
                                      eh * 256 + (isub + 1) * 128]
                          O("tensor", lambda e, pg=pg, lhs=lhs, eh=eh: e.matmul(
                              pg[:], lhs, xnfm[:, eh * 512:(eh + 1) * 512],
                              start=(eh == 0), stop=(eh == 15)),
                            reads=[gkey, "RE"], writes=[pgk])
                      for eh in range(16):
                          lhs = uring[:, eh * 256 + isub * 128:
                                      eh * 256 + (isub + 1) * 128]
                          O("tensor", lambda e, pu=pu, lhs=lhs, eh=eh: e.matmul(
                              pu[:], lhs, xnfm[:, eh * 512:(eh + 1) * 512],
                              start=(eh == 0), stop=(eh == 15)),
                            reads=[ukey, "RE"], writes=[puk])
                      O("tensor", lambda e, pm=pm, blk=blk, tn=tn: e.matmul(
                          pm[:], oh_sb[0:64, blk * 128:(blk + 1) * 128],
                          mask_fm[0:64, tn * 512:(tn + 1) * 512],
                          start=True, stop=True),
                        reads=["oh", "maskfm"], writes=[pmk])
                      O("scalar", lambda e, pg=pg: e.activation(
                          gch[:], pg[:], AF.Silu),
                        reads=[pgk], writes=["RKg"])
                      O("vector", lambda v, pu=pu: v.tensor_tensor(
                          t3[:], gch[:], pu[:], AL.mult),
                        reads=["RKg", puk], writes=["RI2a"])
                      hdst = hid[blk // 16][:, (blk % 16) * 512:
                                            (blk % 16 + 1) * 512]
                      hkey = ("RC", "RC", "RD1", "RD2")[blk // 16]
                      O("vector", lambda v, pm=pm, hdst=hdst: v.tensor_tensor(
                          hdst, t3[:], pm[:], AL.mult),
                        reads=["RI2a", pmk], writes=[hkey])

            if bstage < 6:
                  continue

            ffn_chunks(0, 16)
            for ec in range(8):
                ring = (wr0a, wr0b)[ec % 2]
                rkey = ("W0A", "W0B")[ec % 2]
                rfam = ("w0a", "w0b")[ec % 2]
                wsrc = WO[ec]
                O("gpsimd", lambda e, wsrc=wsrc, ring=ring: e.dma_start(
                    out=ring[:], in_=wsrc),
                  writes=[rkey], dma=rfam)
                for e1 in range(2):
                    ei = ec * 2 + e1
                    p = ps[ei % 4]
                    pk = PSK[ei % 4]
                    for hh in range(16):
                        lhs = ring[:, hh * 256 + e1 * 128:hh * 256 + (e1 + 1) * 128]
                        O("tensor", lambda e, p=p, lhs=lhs, hh=hh: e.matmul(
                            p[:], lhs, ybuf[:, hh * 512:(hh + 1) * 512],
                            start=(hh == 0), stop=(hh == 15)),
                          reads=[rkey, "RD2"], writes=[pk])
                    O("scalar", lambda e, p=p, ei=ei: e.activation(
                        aos[:, ei * 512:(ei + 1) * 512], p[:], AF.Identity,
                        bias=bo_c[:, ei:ei + 1], scale=1.0),
                      reads=[pk, "bo"], writes=["RA"])

            if bstage < 5:
                continue
            ffn_chunks(16, 32)

            for d in range(16):
                dring = (wr0, wr1)[d % 2]
                dkeys = (["W0A", "W0B"], ["W1A", "W1B"])[d % 2]
                dfam = ("dna", "dnb")[d % 2]
                dsrc = DTB[d]
                O("gpsimd", lambda e, dsrc=dsrc, dring=dring: e.dma_start(
                    out=dring[:], in_=dsrc),
                  reads=[f"tdt{d}"], writes=dkeys, dma=dfam)
                xsrc = XT[d * 128:(d + 1) * 128, nto:nto + 512]
                xj = (xtj0, xtj1)[d % 2]
                O("sync", lambda e, xsrc=xsrc, xj=xj: e.dma_start(
                    out=xj[:], in_=xsrc),
                  writes=[("RI1a", "RI1b")[d % 2]], dma=("xtj0", "xtj1")[d % 2])
                p = ps[d % 4]
                pk = PSK[d % 4]
                for ih in range(64):
                    hsl = hid[ih // 16][:, (ih % 16) * 512:(ih % 16 + 1) * 512]
                    hkey = ("RC", "RC", "RD1", "RD2")[ih // 16]
                    O("tensor", lambda e, p=p, ih=ih, hsl=hsl, dring=dring:
                      e.matmul(p[:], dring[:, ih * 128:(ih + 1) * 128], hsl,
                               start=(ih == 0), stop=(ih == 63)),
                      reads=dkeys + [hkey], writes=[pk])
                O("scalar", lambda e, p=p: e.activation(gch[:], p[:], AF.Silu),
                  reads=[pk], writes=["RKg"])
                oc = och[d % 2]
                ok = ("RKo0", "RKo1")[d % 2]
                xjk = ("RI1a", "RI1b")[d % 2]
                O("vector", lambda v, oc=oc, d=d: v.tensor_tensor(
                    oc[:], aos[:, d * 512:(d + 1) * 512], gch[:], AL.mult),
                  reads=["RA", "RKg"], writes=[ok])
                O("vector", lambda v, oc=oc, xj=xj: v.tensor_tensor(
                    oc[:], oc[:], xj[:], AL.add),
                  reads=[ok, xjk], writes=[ok])
                odst = OUT[d * 128:(d + 1) * 128, nto:nto + 512]
                O("sync", lambda e, odst=odst, oc=oc: e.dma_start(
                    out=odst, in_=oc[:]),
                  reads=[ok], writes=[f"outd{d % 2}"], dma=("out0", "out1")[d % 2])

        blk = ctx.enter_context(nc.Block())
        fams = tuple(f for f in ("out0", "out1") if any(o["dma"] == f for o in s.ops))
        s.build(blk, final_dma_fams=fams)

    return nc, s


_CACHE = {}


def make_in_maps(inputs, ncores=8):
    f32 = np.float32
    x = np.ascontiguousarray(np.asarray(inputs["L0_attn_x"], f32))
    def chunkw(w, nch, cw):
        w = np.asarray(w, f32)
        kh = w.shape[0] // 128
        return np.ascontiguousarray(
            w.reshape(kh, 128, nch, cw).transpose(2, 1, 0, 3).reshape(
                nch, 128, kh * cw))

    wqk = chunkw(inputs["L0_attn_Wqk"], 8, 256)
    wv = chunkw(inputs["L0_attn_Wv"], 8, 256)
    wu = chunkw(inputs["L0_attn_Wu"], 8, 256)
    wo = chunkw(inputs["L0_attn_Wo"], 8, 256)
    gateT = chunkw(np.asarray(inputs["L0_ffn_gate_w"], f32).T, 32, 256)
    upT = chunkw(np.asarray(inputs["L0_ffn_up_w"], f32).T, 32, 256)
    downT = chunkw(np.asarray(inputs["L0_ffn_down_w"], f32).T, 16, 128)
    r1T = np.ascontiguousarray(np.asarray(inputs["L0_ffn_router_w1"], f32).T)
    r2T = np.ascontiguousarray(np.asarray(inputs["L0_ffn_router_w2"], f32).T)

    def col(v):
        return np.ascontiguousarray(np.asarray(v, f32).reshape(16, 128).T)

    consts = {
        "WQK": wqk, "WV": wv, "WU": wu, "WO": wo,
        "GT": gateT, "UT": upT, "DT": downT, "R1T": r1T, "R2T": r2T,
        "BQK": col(inputs["L0_attn_bqk"]), "BV": col(inputs["L0_attn_bv"]),
        "BU": col(inputs["L0_attn_bu"]), "BO": col(inputs["L0_attn_bo"]),
        "LNG": col(inputs["L0_attn_ln_g"]), "LNB": col(inputs["L0_attn_ln_b"]),
        "IDB": np.eye(128, dtype=ml_dtypes.bfloat16),
        "IDF": np.eye(128, dtype=f32),
        "ONEC": np.ones((128, 1), dtype=ml_dtypes.bfloat16),
        "ONERF": np.ones((1, 128), dtype=f32),
        "OH": np.kron(np.eye(64), np.ones((1, 128))).astype(ml_dtypes.bfloat16),
    }

    in_maps = []
    for c in range(ncores):
        b, h = c >> 1, c & 1
        own = x[b, h * 2048:(h + 1) * 2048]
        m = dict(consts)
        m["X"] = np.ascontiguousarray(own)
        m["XT"] = np.ascontiguousarray(own.T)
        in_maps.append(m)
    return in_maps


def kernel(**inputs):
    in_maps = make_in_maps(inputs)
    if "nc" not in _CACHE:
        _CACHE["nc"] = build_program()
    nc, _ = _CACHE["nc"]

    res = run_bass_kernel_spmd(nc, in_maps, core_ids=list(range(8)))

    out = np.empty((4, 4096, 2048), np.float32)
    for c, r in enumerate(res.results):
        b, h = c >> 1, c & 1
        out[b, h * 2048:(h + 1) * 2048, :] = r["OUT"].T
    return out


if __name__ == "__main__":
    nc, s = build_program()
    print("ops:", len(s.ops), "incs:", s.n_incs)
